# version 07
# speedup vs baseline: 1.8765x; 1.0638x over previous
import numpy as np
import ml_dtypes
import contextlib
import math
import concourse.bass as bass
import concourse.mybir as mybir
from concourse.bass_utils import run_bass_kernel_spmd
from concourse.bass_core_rust import IndirectOffsetOnAxis

F32 = mybir.dt.float32
BF16 = mybir.dt.bfloat16
I32 = mybir.dt.int32
AF = mybir.ActivationFunctionType
ALU = mybir.AluOpType
AX = mybir.AxisListType.X

B = 2
S = 8192
DM = 1024
EXT = 2176
QEXT = 2064
NT = EXT // 128
NFT = S // 128
SHIFT = 2052
H0, DQK = 8, 16
CAP = 2560
DBG_D = False
A2_SIM_RELU = False
EPS = 1e-5


class Prog:
    def __init__(self, scratch=16384):
        import os
        self.nc = bass.Bass("TRN2", target_bir_lowering=False, debug=False,
                            dynamic_dma_scratch_size=scratch,
                            detect_race_conditions=(os.environ.get("MKRACE", "0") == "1"))
        self.ctx = contextlib.ExitStack()
        self.scope = None
        self._ns = 0

    def push_scope(self):
        assert self.scope is None
        self.scope = contextlib.ExitStack()

    def pop_scope(self):
        self.scope.close()
        self.scope = None

    def _stk(self):
        return self.scope if self.scope is not None else self.ctx

    def sb(self, name, shape, dt=F32):
        return self._stk().enter_context(self.nc.sbuf_tensor(name, shape, dt))

    def ps(self, name, shape, dt=F32):
        return self._stk().enter_context(self.nc.psum_tensor(name, shape, dt))

    def sem(self, name=None):
        self._ns += 1
        return self.ctx.enter_context(self.nc.semaphore(name or f"s{self._ns}"))

    def block(self):
        if getattr(self, "_blk", None) is not None:
            self._blk.__exit__(None, None, None)
        cm = self.nc.Block()
        self._blk = cm
        return cm.__enter__()

    def din(self, name, shape, dt=F32):
        return self.nc.dram_tensor(name, shape, dt, kind="ExternalInput").ap()

    def dout(self, name, shape, dt=F32):
        return self.nc.dram_tensor(name, shape, dt, kind="ExternalOutput").ap()

    def dint(self, name, shape, dt=F32):
        return self.nc.dram_tensor(name, shape, dt, kind="Internal")

    def done(self):
        if getattr(self, "_blk", None) is not None:
            self._blk.__exit__(None, None, None)
            self._blk = None
        self.ctx.close()
        return self.nc





class Chain:

    def __init__(self, p):
        self.p = p
        self.sem = p.sem()
        self._gsem = None
        self.ops = []

    @property
    def gsem(self):
        if self._gsem is None:
            self._gsem = self.p.sem()
        return self._gsem

    def add(self, engine, fn, inc=1):
        self.ops.append((engine, fn, inc))

    def dma(self, engine, fn):
        self.add(engine, fn, 16)

    def emit(self, blk):
        n = len(self.ops)
        cum_m = [0] * (n + 1)
        cum_g = [0] * (n + 1)
        for i, (e, _, inc) in enumerate(self.ops):
            cum_m[i + 1] = cum_m[i] + (0 if e == "gpsimd" else inc)
            cum_g[i + 1] = cum_g[i] + (inc if e == "gpsimd" else 0)
        tot_m, tot_g = cum_m[n], cum_g[n]
        wm, wg = [], []
        for i, (e, _, _) in enumerate(self.ops):
            j = i - 1
            if e == "tensor":
                while j >= 0 and self.ops[j][0] == "tensor":
                    j -= 1
            wm.append(cum_m[j + 1] if j >= 0 else 0)
            wg.append(cum_g[j + 1] if j >= 0 else 0)

        for eng_name in ("sync", "scalar", "vector", "tensor", "gpsimd"):
            if not any(e == eng_name for (e, _, _) in self.ops):
                continue

            def mk(eng_name):
                def body(eng):
                    last_m = last_g = 0
                    for idx, (e, fn, inc) in enumerate(self.ops):
                        if e != eng_name:
                            continue
                        if wm[idx] > last_m:
                            eng.wait_ge(self.sem, wm[idx])
                            last_m = wm[idx]
                        if wg[idx] > last_g:
                            eng.wait_ge(self.gsem, wg[idx])
                            last_g = wg[idx]
                        instr = fn(eng)
                        tgt = self.gsem if e == "gpsimd" else self.sem
                        if instr is not None:
                            instr.then_inc(tgt, inc)
                        else:
                            eng.sem_inc(tgt, inc)
                    return None
                return body

            getattr(blk, eng_name)(mk(eng_name))

        def fin(eng):
            eng.wait_ge(self.sem, tot_m)
            if tot_g:
                eng.wait_ge(self.gsem, tot_g)
        blk.sync(fin)



class Chain2:

    def __init__(self, p):
        self.p = p
        self.ops = []

    def add(self, engine, fn, reads=(), writes=(), inc=1):
        self.ops.append((engine, fn, inc, tuple(reads), tuple(writes)))

    def dma(self, engine, fn, reads=(), writes=()):
        self.add(engine, fn, reads, writes, inc=16)

    def emit(self, blk):
        n = len(self.ops)
        engines = []
        for (e, _, _, _, _) in self.ops:
            if e not in engines:
                engines.append(e)
        sems = {e: self.p.sem() for e in engines}
        cum = [None] * n
        run = {e: 0 for e in engines}
        for i, (e, _, inc, _, _) in enumerate(self.ops):
            run[e] += inc
            cum[i] = run[e]
        totals = dict(run)
        last_writer = {}
        readers = {}
        deps = [dict() for _ in range(n)]

        def need(i, j):
            ej = self.ops[j][0]
            ei = self.ops[i][0]
            if ei == "tensor" and ej == "tensor":
                return
            d = deps[i]
            if d.get(ej, -1) < cum[j]:
                d[ej] = cum[j]
        for i, (e, _, inc, rd, wr) in enumerate(self.ops):
            for b in rd:
                j = last_writer.get(b)
                if j is not None:
                    need(i, j)
            for b in wr:
                j = last_writer.get(b)
                if j is not None:
                    need(i, j)
                for j in readers.get(b, ()):
                    need(i, j)
            for b in rd:
                readers.setdefault(b, []).append(i)
            for b in wr:
                last_writer[b] = i
                readers[b] = []

        for eng_name in ("sync", "scalar", "vector", "tensor", "gpsimd"):
            if eng_name not in engines:
                continue

            def mk(eng_name):
                def body(eng):
                    last = {e: 0 for e in engines}
                    for i, (e, fn, inc, _, _) in enumerate(self.ops):
                        if e != eng_name:
                            continue
                        for ej, v in sorted(deps[i].items()):
                            if v > last[ej]:
                                eng.wait_ge(sems[ej], v)
                                last[ej] = v
                        instr = fn(eng)
                        if instr is not None:
                            instr.then_inc(sems[e], inc)
                        else:
                            eng.sem_inc(sems[e], inc)
                    return None
                return body

            getattr(blk, eng_name)(mk(eng_name))

        def fin(eng):
            for e in engines:
                eng.wait_ge(sems[e], totals[e])
        blk.sync(fin)



def emit_ln_stream(p, blk, tag, src_ap, n_tiles, store_f32, store_bf16):
    xin = [p.sb(f"{tag}_xin{i}", [128, DM]) for i in range(2)]
    xsq = [p.sb(f"{tag}_xsq{i}", [128, DM]) for i in range(2)]
    xnf = [p.sb(f"{tag}_xnf{i}", [128, DM]) for i in range(2)]
    xnb = [p.sb(f"{tag}_xnb{i}", [128, DM], BF16) for i in range(2)]
    s1 = [p.sb(f"{tag}_s1{i}", [128, 1]) for i in range(2)]
    s2 = [p.sb(f"{tag}_s2{i}", [128, 1]) for i in range(2)]
    sq1 = [p.sb(f"{tag}_sq1{i}", [128, 1]) for i in range(2)]
    v1 = [p.sb(f"{tag}_v1{i}", [128, 1]) for i in range(2)]
    sd = [p.sb(f"{tag}_sd{i}", [128, 1]) for i in range(2)]
    rstd = [p.sb(f"{tag}_r{i}", [128, 1]) for i in range(2)]
    nmr = [p.sb(f"{tag}_nmr{i}", [128, 1]) for i in range(2)]
    epsb = p.sb(f"{tag}_eps", [128, 1])

    ch = Chain2(p)
    ch.add("vector", lambda e: e.memset(epsb[:], EPS), writes=("eps",))

    def stage_a(t):
        q = t % 2
        ch.dma("sync", lambda e, t=t, q=q: e.dma_start(
            out=xin[q][:], in_=src_ap[128 * t:128 * (t + 1), :]),
            writes=(f"xin{q}",))
        ch.add("scalar", lambda e, q=q: e.activation(
            out=xsq[q][:], in_=xin[q][:], func=AF.Square,
            accum_out=s2[q][:]),
            reads=(f"xin{q}",), writes=(f"xsq{q}", f"s2{q}"))
        ch.add("vector", lambda e, q=q: e.reduce_sum(out=s1[q][:], in_=xin[q][:],
                                                     axis=AX),
               reads=(f"xin{q}",), writes=(f"s1{q}",))
        ch.add("vector", lambda e, q=q: e.tensor_tensor(
            out=sq1[q][:], in0=s1[q][:], in1=s1[q][:], op=ALU.mult),
            reads=(f"s1{q}",), writes=(f"sq1{q}",))
        ch.add("vector", lambda e, q=q: e.scalar_tensor_tensor(
            out=v1[q][:], in0=s2[q][:], scalar=float(DM), in1=sq1[q][:],
            op0=ALU.mult, op1=ALU.subtract),
            reads=(f"s2{q}", f"sq1{q}"), writes=(f"v1{q}",))
        ch.add("scalar", lambda e, q=q: e.activation(
            out=sd[q][:], in_=v1[q][:], func=AF.Sqrt,
            scale=1.0 / (DM * DM), bias=epsb[:]),
            reads=(f"v1{q}", "eps"), writes=(f"sd{q}",))

    def stage_b(t):
        q = t % 2
        ch.add("vector", lambda e, q=q: e.reciprocal(out=rstd[q][:],
                                                     in_=sd[q][:]),
               reads=(f"sd{q}",), writes=(f"rstd{q}",))
        ch.add("vector", lambda e, q=q: e.scalar_tensor_tensor(
            out=nmr[q][:], in0=s1[q][:], scalar=-1.0 / DM, in1=rstd[q][:],
            op0=ALU.mult, op1=ALU.mult),
            reads=(f"s1{q}", f"rstd{q}"), writes=(f"nmr{q}",))
        ch.add("scalar", lambda e, q=q: e.activation(
            out=xnf[q][:], in_=xin[q][:], func=AF.Identity,
            scale=rstd[q][:], bias=nmr[q][:]),
            reads=(f"xin{q}", f"rstd{q}", f"nmr{q}"), writes=(f"xnf{q}",))
        if store_bf16 is not None:
            ch.add("vector", lambda e, q=q: e.tensor_copy(out=xnb[q][:],
                                                          in_=xnf[q][:]),
                   reads=(f"xnf{q}",), writes=(f"xnb{q}",))
        if store_f32 is not None:
            ch.dma("sync", lambda e, t=t, q=q: e.dma_start(
                out=store_f32[128 * t:128 * (t + 1), :], in_=xnf[q][:]),
                reads=(f"xnf{q}",))
        if store_bf16 is not None:
            ch.dma("sync", lambda e, t=t, q=q: e.dma_start(
                out=store_bf16[128 * t:128 * (t + 1), :], in_=xnb[q][:]),
                reads=(f"xnb{q}",))

    for t in range(n_tiles):
        stage_a(t)
        if t >= 1:
            stage_b(t - 1)
    stage_b(n_tiles - 1)
    ch.emit(blk)




def emit_proj_stream(p, blk, tag, src_dram, n_tiles, n_valid, wT_sb, bias_sb,
                     identb_sb, kp_out, kp_w, nmax_sb,
                     vp_out=None, wvT_sb=None, bv_sb=None, anorm_out=None,
                     bvb_row=None, ones1=None):
    has_v = vp_out is not None

    xnb = [p.sb(f"{tag}_xnb{i}", [128, DM], BF16) for i in range(2)]
    xfm = [p.sb(f"{tag}_xfm{i}", [128, DM], BF16) for i in range(2)]
    ktm = [p.sb(f"{tag}_ktm{i}", [128, 128]) for i in range(2)]
    ksq = [p.sb(f"{tag}_ksq{i}", [128, 128]) for i in range(2)]
    knt = [p.sb(f"{tag}_knt{i}", [128, 8]) for i in range(2)]
    kbf = [p.sb(f"{tag}_kbf{i}", [128, 128], BF16) for i in range(2)]
    kT = [p.sb(f"{tag}_kT{i}", [128, 128], BF16) for i in range(2)]
    vsb = [p.sb(f"{tag}_v{i}", [128, DM], BF16) for i in range(2)] if has_v else None
    psT = [p.ps(f"{tag}_psT{i}", [128, 1024], BF16) for i in range(2)]
    psK = p.ps(f"{tag}_psK", [128, 128], F32)
    psKT = p.ps(f"{tag}_psKT", [128, 128], BF16)
    psV = [p.ps(f"{tag}_psV{i}", [128, 512], F32) for i in range(2)] if has_v else None

    kp3 = kp_out.rearrange("d (h t) -> h d t", h=8)

    ch = Chain2(p)
    ch.add("vector", lambda e: e.memset(nmax_sb[:], 0.0), writes=("nmax",))
    for t in range(n_tiles):
        q = t % 2
        ch.dma("sync", lambda e, t=t, q=q: e.dma_start(
            out=xnb[q][:], in_=src_dram[128 * t:128 * (t + 1), :]),
            writes=(f"xnb{q}",))
        for c in range(8):
            ch.add("tensor", lambda e, q=q, c=c: e.transpose(
                psT[q][:, 128 * c:128 * (c + 1)],
                xnb[q][:, 128 * c:128 * (c + 1)], identb_sb[:]),
                reads=(f"xnb{q}", "ident"), writes=(f"psT{q}",))
        ch.add("vector", lambda e, q=q: e.tensor_copy(out=xfm[q][:],
                                                      in_=psT[q][:]),
               reads=(f"psT{q}",), writes=(f"xfm{q}",))
        for c in range(8):
            ch.add("tensor", lambda e, q=q, c=c: e.matmul(
                psK[:, 0:128], xfm[q][:, 128 * c:128 * (c + 1)],
                wT_sb[:, 128 * c:128 * (c + 1)],
                start=(c == 0), stop=(c == 7)),
                reads=(f"xfm{q}", "wT"), writes=("psK",))
        if has_v:
            ch.add("tensor", lambda e: e.matmul(
                psV[0][:, 0:512], ones1[:], bvb_row[:, 0:512],
                start=True, stop=False), reads=("ones1", "bvb"),
                writes=("psV0",))
            ch.add("tensor", lambda e: e.matmul(
                psV[1][:, 0:512], ones1[:], bvb_row[:, 512:1024],
                start=True, stop=False), reads=("ones1", "bvb"),
                writes=("psV1",))
            for c in range(8):
                for hf in range(2):
                    ch.add("tensor", lambda e, q=q, c=c, hf=hf: e.matmul(
                        psV[hf][:, 0:512],
                        xfm[q][:, 128 * c:128 * (c + 1)],
                        wvT_sb[:, 1024 * c + 512 * hf:1024 * c + 512 * (hf + 1)],
                        start=False, stop=(c == 7)),
                        reads=(f"xfm{q}", "wvT"), writes=(f"psV{hf}",))
        ch.add("vector", lambda e, q=q: e.tensor_tensor(
            out=ktm[q][:], in0=psK[:, 0:128], in1=bias_sb[:], op=ALU.add),
            reads=("psK", "bias"), writes=(f"ktm{q}",))
        ch.add("scalar", lambda e, q=q: e.activation(
            out=ksq[q][:], in_=ktm[q][:], func=AF.Square),
            reads=(f"ktm{q}",), writes=(f"ksq{q}",))
        ch.add("vector", lambda e, q=q: e.reduce_sum(
            out=knt[q][:].rearrange("p (a b) -> p a b", b=1),
            in_=ksq[q][:].rearrange("p (a b) -> p a b", b=16), axis=AX),
            reads=(f"ksq{q}",), writes=(f"knt{q}",))
        if t < n_valid:
            ch.add("vector", lambda e, q=q: e.tensor_tensor(
                out=nmax_sb[:], in0=nmax_sb[:], in1=knt[q][:], op=ALU.max),
                reads=(f"knt{q}", "nmax"), writes=("nmax",))
        ch.add("vector", lambda e, q=q: e.tensor_copy(out=kbf[q][:],
                                                      in_=ktm[q][:]),
               reads=(f"ktm{q}",), writes=(f"kbf{q}",))
        ch.add("tensor", lambda e, q=q: e.transpose(
            psKT[:, 0:128], kbf[q][:], identb_sb[:]),
            reads=(f"kbf{q}", "ident"), writes=("psKT",))
        if has_v:
            for hf in range(2):
                ch.add("vector", lambda e, q=q, hf=hf: e.tensor_copy(
                    out=vsb[q][:, 512 * hf:512 * (hf + 1)],
                    in_=psV[hf][:, 0:512]),
                    reads=(f"psV{hf}",), writes=(f"vsb{q}",))
            ch.dma("sync", lambda e, t=t, q=q: e.dma_start(
                out=vp_out[128 * t:128 * (t + 1), :], in_=vsb[q][:]),
                reads=(f"vsb{q}",))
        ch.add("vector", lambda e, q=q: e.tensor_copy(out=kT[q][:],
                                                      in_=psKT[:, 0:128]),
               reads=("psKT",), writes=(f"kT{q}",))
        w = min(128, kp_w - 128 * t)
        if anorm_out is not None and w > 0:
            for h in range(8):
                ch.dma("sync", lambda e, t=t, q=q, h=h, w=w: e.dma_start(
                    out=anorm_out[h:h + 1, 128 * t:128 * t + w],
                    in_=knt[q][0:w, h:h + 1]), reads=(f"knt{q}",))
        if w > 0:
            ch.dma("sync", lambda e, t=t, q=q, w=w: e.dma_start(
                out=kp3[:, :, 128 * t:128 * t + w], in_=kT[q][:, 0:w]),
                reads=(f"kT{q}",))
    ch.emit(blk)



def emit_ln_proj(p, blk, tag, src_ap, n_tiles, n_valid, store_f32, wT_sb,
                 bias_sb, identb_sb, kp_out, kp_w, nmax_sb,
                 vp_out=None, wvT_sb=None, bv_sb=None, anorm_out=None,
                 bvb_row=None, ones1=None):
    has_v = vp_out is not None
    xin = [p.sb(f"{tag}_xin{i}", [128, DM]) for i in range(2)]
    xsq = [p.sb(f"{tag}_xsq{i}", [128, DM]) for i in range(2)]
    xnf = [p.sb(f"{tag}_xnf{i}", [128, DM]) for i in range(2)]
    xnb = [p.sb(f"{tag}_xnb{i}", [128, DM], BF16) for i in range(2)]
    s1 = [p.sb(f"{tag}_s1{i}", [128, 1]) for i in range(2)]
    s2 = [p.sb(f"{tag}_s2{i}", [128, 1]) for i in range(2)]
    sq1 = [p.sb(f"{tag}_sq1{i}", [128, 1]) for i in range(2)]
    v1 = [p.sb(f"{tag}_v1{i}", [128, 1]) for i in range(2)]
    sd = [p.sb(f"{tag}_sd{i}", [128, 1]) for i in range(2)]
    rstd = [p.sb(f"{tag}_r{i}", [128, 1]) for i in range(2)]
    nmr = [p.sb(f"{tag}_nmr{i}", [128, 1]) for i in range(2)]
    epsb = p.sb(f"{tag}_eps", [128, 1])
    xfm = [p.sb(f"{tag}_xfm{i}", [128, DM], BF16) for i in range(2)]
    ktm = [p.sb(f"{tag}_ktm{i}", [128, 128]) for i in range(2)]
    ksq = [p.sb(f"{tag}_ksq{i}", [128, 128]) for i in range(2)]
    knt = [p.sb(f"{tag}_knt{i}", [128, 8]) for i in range(2)]
    kbf = [p.sb(f"{tag}_kbf{i}", [128, 128], BF16) for i in range(2)]
    kT = [p.sb(f"{tag}_kT{i}", [128, 128], BF16) for i in range(2)]
    vsb = [p.sb(f"{tag}_v{i}", [128, DM], BF16) for i in range(2)] if has_v else None
    psT = [p.ps(f"{tag}_psT{i}", [128, 1024], BF16) for i in range(2)]
    psK = p.ps(f"{tag}_psK", [128, 128], F32)
    psKT = p.ps(f"{tag}_psKT", [128, 128], BF16)
    psV = [p.ps(f"{tag}_psV{i}", [128, 512], F32) for i in range(2)] if has_v else None
    kp3 = kp_out.rearrange("d (h t) -> h d t", h=8)

    ch = Chain2(p)
    ch.add("vector", lambda e: e.memset(epsb[:], EPS), writes=("eps",))
    ch.add("vector", lambda e: e.memset(nmax_sb[:], 0.0), writes=("nmax",))

    def stage_a(t):
        q = t % 2
        ch.dma("sync", lambda e, t=t, q=q: e.dma_start(
            out=xin[q][:], in_=src_ap[128 * t:128 * (t + 1), :]),
            writes=(f"xin{q}",))
        ch.add("scalar", lambda e, q=q: e.activation(
            out=xsq[q][:], in_=xin[q][:], func=AF.Square, accum_out=s2[q][:]),
            reads=(f"xin{q}",), writes=(f"xsq{q}", f"s2{q}"))
        ch.add("vector", lambda e, q=q: e.reduce_sum(out=s1[q][:], in_=xin[q][:],
                                                     axis=AX),
               reads=(f"xin{q}",), writes=(f"s1{q}",))
        ch.add("vector", lambda e, q=q: e.tensor_tensor(
            out=sq1[q][:], in0=s1[q][:], in1=s1[q][:], op=ALU.mult),
            reads=(f"s1{q}",), writes=(f"sq1{q}",))
        ch.add("vector", lambda e, q=q: e.scalar_tensor_tensor(
            out=v1[q][:], in0=s2[q][:], scalar=float(DM), in1=sq1[q][:],
            op0=ALU.mult, op1=ALU.subtract),
            reads=(f"s2{q}", f"sq1{q}"), writes=(f"v1{q}",))
        ch.add("scalar", lambda e, q=q: e.activation(
            out=sd[q][:], in_=v1[q][:], func=AF.Sqrt,
            scale=1.0 / (DM * DM), bias=epsb[:]),
            reads=(f"v1{q}", "eps"), writes=(f"sd{q}",))

    def stage_b(t):
        q = t % 2
        ch.add("vector", lambda e, q=q: e.reciprocal(out=rstd[q][:], in_=sd[q][:]),
               reads=(f"sd{q}",), writes=(f"rstd{q}",))
        ch.add("vector", lambda e, q=q: e.scalar_tensor_tensor(
            out=nmr[q][:], in0=s1[q][:], scalar=-1.0 / DM, in1=rstd[q][:],
            op0=ALU.mult, op1=ALU.mult),
            reads=(f"s1{q}", f"rstd{q}"), writes=(f"nmr{q}",))
        ch.add("scalar", lambda e, q=q: e.activation(
            out=xnf[q][:], in_=xin[q][:], func=AF.Identity,
            scale=rstd[q][:], bias=nmr[q][:]),
            reads=(f"xin{q}", f"rstd{q}", f"nmr{q}"), writes=(f"xnf{q}",))
        ch.add("vector", lambda e, q=q: e.tensor_copy(out=xnb[q][:], in_=xnf[q][:]),
               reads=(f"xnf{q}",), writes=(f"xnb{q}",))
        if store_f32 is not None:
            ch.dma("sync", lambda e, t=t, q=q: e.dma_start(
                out=store_f32[128 * t:128 * (t + 1), :], in_=xnf[q][:]),
                reads=(f"xnf{q}",))

    def stage_p(t):
        q = t % 2
        for c in range(8):
            ch.add("tensor", lambda e, q=q, c=c: e.transpose(
                psT[q][:, 128 * c:128 * (c + 1)],
                xnb[q][:, 128 * c:128 * (c + 1)], identb_sb[:]),
                reads=(f"xnb{q}", "ident"), writes=(f"psT{q}",))
        ch.add("vector", lambda e, q=q: e.tensor_copy(out=xfm[q][:], in_=psT[q][:]),
               reads=(f"psT{q}",), writes=(f"xfm{q}",))
        for c in range(8):
            ch.add("tensor", lambda e, q=q, c=c: e.matmul(
                psK[:, 0:128], xfm[q][:, 128 * c:128 * (c + 1)],
                wT_sb[:, 128 * c:128 * (c + 1)],
                start=(c == 0), stop=(c == 7)),
                reads=(f"xfm{q}", "wT"), writes=("psK",))
        if has_v:
            ch.add("tensor", lambda e: e.matmul(
                psV[0][:, 0:512], ones1[:], bvb_row[:, 0:512],
                start=True, stop=False), reads=("ones1", "bvb"),
                writes=("psV0",))
            ch.add("tensor", lambda e: e.matmul(
                psV[1][:, 0:512], ones1[:], bvb_row[:, 512:1024],
                start=True, stop=False), reads=("ones1", "bvb"),
                writes=("psV1",))
            for c in range(8):
                for hf in range(2):
                    ch.add("tensor", lambda e, q=q, c=c, hf=hf: e.matmul(
                        psV[hf][:, 0:512],
                        xfm[q][:, 128 * c:128 * (c + 1)],
                        wvT_sb[:, 1024 * c + 512 * hf:1024 * c + 512 * (hf + 1)],
                        start=False, stop=(c == 7)),
                        reads=(f"xfm{q}", "wvT"), writes=(f"psV{hf}",))
        ch.add("vector", lambda e, q=q: e.tensor_tensor(
            out=ktm[q][:], in0=psK[:, 0:128], in1=bias_sb[:], op=ALU.add),
            reads=("psK", "bias"), writes=(f"ktm{q}",))
        ch.add("scalar", lambda e, q=q: e.activation(
            out=ksq[q][:], in_=ktm[q][:], func=AF.Square),
            reads=(f"ktm{q}",), writes=(f"ksq{q}",))
        ch.add("vector", lambda e, q=q: e.reduce_sum(
            out=knt[q][:].rearrange("p (a b) -> p a b", b=1),
            in_=ksq[q][:].rearrange("p (a b) -> p a b", b=16), axis=AX),
            reads=(f"ksq{q}",), writes=(f"knt{q}",))
        if t < n_valid:
            ch.add("vector", lambda e, q=q: e.tensor_tensor(
                out=nmax_sb[:], in0=nmax_sb[:], in1=knt[q][:], op=ALU.max),
                reads=(f"knt{q}", "nmax"), writes=("nmax",))
        ch.add("vector", lambda e, q=q: e.tensor_copy(out=kbf[q][:],
                                                      in_=ktm[q][:]),
               reads=(f"ktm{q}",), writes=(f"kbf{q}",))
        ch.add("tensor", lambda e, q=q: e.transpose(
            psKT[:, 0:128], kbf[q][:], identb_sb[:]),
            reads=(f"kbf{q}", "ident"), writes=("psKT",))
        if has_v:
            for hf in range(2):
                ch.add("vector", lambda e, q=q, hf=hf: e.tensor_copy(
                    out=vsb[q][:, 512 * hf:512 * (hf + 1)],
                    in_=psV[hf][:, 0:512]),
                    reads=(f"psV{hf}",), writes=(f"vsb{q}",))
            ch.dma("sync", lambda e, t=t, q=q: e.dma_start(
                out=vp_out[128 * t:128 * (t + 1), :], in_=vsb[q][:]),
                reads=(f"vsb{q}",))
        ch.add("vector", lambda e, q=q: e.tensor_copy(out=kT[q][:],
                                                      in_=psKT[:, 0:128]),
               reads=("psKT",), writes=(f"kT{q}",))
        w = min(128, kp_w - 128 * t)
        if anorm_out is not None and w > 0:
            ch.dma("sync", lambda e, t=t, q=q, w=w: e.dma_start(
                out=anorm_out[128 * t:128 * t + w, :],
                in_=knt[q][0:w, :]), reads=(f"knt{q}",))
        if w > 0:
            ch.dma("sync", lambda e, t=t, q=q, w=w: e.dma_start(
                out=kp3[:, :, 128 * t:128 * t + w], in_=kT[q][:, 0:w]),
                reads=(f"kT{q}",))

    for t in range(n_tiles):
        stage_a(t)
        if t >= 1:
            stage_b(t - 1)
            stage_p(t - 1)
    stage_b(n_tiles - 1)
    stage_p(n_tiles - 1)
    ch.emit(blk)




def build_a1():
    p = Prog()
    x_full = p.din("x_full", [S, DM])
    x_ext = p.din("x_ext", [EXT, DM])
    wq = p.din("wq", [128, DM])
    wk = p.din("wk", [128, DM])
    wv = p.din("wv", [DM, DM])
    bq_b = p.din("bq_b", [128, 128])
    bk_b = p.din("bk_b", [128, 128])
    bv_b = p.din("bv_b", [128, DM])
    identb = p.din("identb", [128, 128], BF16)

    o_xn = p.dout("o_xn", [EXT, DM])
    o_qp = p.dout("o_qp", [16, H0 * QEXT], BF16)
    o_kp = p.dout("o_kp", [16, H0 * S], BF16)
    o_vp = p.dout("o_vp", [S, DM], BF16)
    o_ch = p.dout("o_ch", [128, 8])
    o_anorm = p.dout("o_anorm", [EXT, 8])
    o_kn = p.dout("o_kn", [8, 1])

    nbounce = p.dint("nbounce", [4096])
    chb = p.dint("chb", [16])

    identb_sb = p.sb("identb_sb", [128, 128], BF16)
    wqT = p.sb("wqT", [128, 8 * 128], BF16)
    wkT = p.sb("wkT", [128, 8 * 128], BF16)
    wvT = p.sb("wvT", [128, 8 * 1024], BF16)
    wq_bf = p.sb("wq_bf", [128, DM], BF16)
    wk_bf = p.sb("wk_bf", [128, DM], BF16)
    wv_bf = p.sb("wv_bf", [128, 8 * 1024], BF16)
    bq_sb = p.sb("bq_sb", [128, 128])
    bk_sb = p.sb("bk_sb", [128, 128])
    bv_sb = p.sb("bv_sb", [128, DM])
    bvb_row = p.sb("bvb_row", [1, DM], BF16)
    ones1 = p.sb("ones1", [1, 128], BF16)
    qnmax = p.sb("qnmax", [128, 8])
    knmax = p.sb("knmax", [128, 8])

    LW = p.sem("LW")

    p.push_scope()
    blk = p.block()
    wstage = p.sb("wstage", [128, 8 * 1024])

    @blk.sync
    def _(sync):
        for i, (dst, srcap) in enumerate([(bq_sb[:], bq_b), (bk_sb[:], bk_b),
                                          (bv_sb[:], bv_b), (identb_sb[:], identb)]):
            sync.wait_ge(LW, 0)
            sync.dma_start(out=dst, in_=srcap).then_inc(LW, 16)
        for c in range(8):
            sync.wait_ge(LW, 0)
            sync.dma_start(out=wstage[:, 1024 * c:1024 * (c + 1)],
                           in_=wv[128 * c:128 * (c + 1), :]).then_inc(LW, 16)

    @blk.vector
    def _(vector):
        vector.wait_ge(LW, 16 * 12)
        vector.tensor_copy(out=wv_bf[:], in_=wstage[:]).then_inc(LW, 1)

    @blk.sync
    def _(sync):
        sync.wait_ge(LW, 16 * 12 + 1)
        sync.dma_start(out=wstage[:, 0:1024], in_=wq).then_inc(LW, 16)
        sync.wait_ge(LW, 16 * 12 + 1)
        sync.dma_start(out=wstage[:, 1024:2048], in_=wk).then_inc(LW, 16)

    @blk.vector
    def _(vector):
        vector.wait_ge(LW, 16 * 14 + 1)
        vector.memset(ones1[:], 1.0)
        vector.tensor_copy(out=bvb_row[:], in_=bv_sb[0:1, :])
        vector.tensor_copy(out=wq_bf[:], in_=wstage[:, 0:1024])
        vector.tensor_copy(out=wk_bf[:], in_=wstage[:, 1024:2048]).then_inc(LW, 1)

    @blk.sync
    def _(sync):
        sync.wait_ge(LW, 16 * 14 + 2)
    p.pop_scope()

    p.push_scope()
    blk = p.block()
    TT0, VE0 = p.sem("TT0"), p.sem("VE0")
    psW = [p.ps(f"psW{i}", [128, 1024], BF16) for i in range(2)]
    jobs = []
    for c in range(8):
        jobs.append((wq_bf[:, 128 * c:128 * (c + 1)], wqT, 128 * c))
    for c in range(8):
        jobs.append((wk_bf[:, 128 * c:128 * (c + 1)], wkT, 128 * c))
    for rc in range(8):
        for cc in range(8):
            jobs.append((wv_bf[:, 1024 * rc + 128 * cc:1024 * rc + 128 * (cc + 1)],
                         wvT, 1024 * cc + 128 * rc))

    @blk.tensor
    def _(tensor):
        for i, (src, dst, off) in enumerate(jobs):
            if i >= 2:
                tensor.wait_ge(VE0, i - 1)
            tensor.transpose(psW[i % 2][:, 0:128], src, identb_sb[:]).then_inc(TT0, 1)

    @blk.vector
    def _(vector):
        for i, (src, dst, off) in enumerate(jobs):
            vector.wait_ge(TT0, i + 1)
            vector.tensor_copy(out=dst[:, off:off + 128],
                               in_=psW[i % 2][:, 0:128]).then_inc(VE0, 1)

    p.pop_scope()

    p.push_scope()
    blk = p.block()
    emit_ln_proj(p, blk, "kv", x_full, NFT, NFT, None, wkT, bk_sb,
                 identb_sb, o_kp, S, knmax,
                 vp_out=o_vp, wvT_sb=wvT, bv_sb=bv_sb,
                 bvb_row=bvb_row, ones1=ones1)
    p.pop_scope()

    p.push_scope()
    blk = p.block()
    emit_ln_proj(p, blk, "q", x_ext, NT, NT, o_xn, wqT, bq_sb,
                 identb_sb, o_qp, QEXT, qnmax, anorm_out=o_anorm)
    p.pop_scope()

    p.push_scope()
    blk = p.block()
    CF = p.sem("CF")
    qn8 = p.sb("qn8", [8, 128])
    kn8 = p.sb("kn8", [8, 128])
    qn1 = p.sb("qn1", [8, 1])
    kn1 = p.sb("kn1", [8, 1])
    ch1 = p.sb("ch1", [8, 1])
    ch1b = p.sb("ch1b", [8, 1])
    ch8 = p.sb("ch8", [1, 8])
    onesf = p.sb("onesf", [1, 128])
    chb_sb = p.sb("chb_sb", [128, 8])
    psCH = p.ps("psCH", [128, 512], F32)

    @blk.sync
    def _(sync):
        sync.dma_start(out=bass.AP(nbounce, 0, [[8, 128], [1, 8]]),
                       in_=qnmax[:]).then_inc(CF, 16)
        sync.dma_start(out=bass.AP(nbounce, 1024, [[8, 128], [1, 8]]),
                       in_=knmax[:]).then_inc(CF, 16)
        sync.wait_ge(CF, 32)
        sync.dma_start(out=qn8[:],
                       in_=bass.AP(nbounce, 0, [[1, 8], [8, 128]]),
                       allow_slow_non_contiguous=True).then_inc(CF, 16)
        sync.dma_start(out=kn8[:],
                       in_=bass.AP(nbounce, 1024, [[1, 8], [8, 128]]),
                       allow_slow_non_contiguous=True).then_inc(CF, 16)

    @blk.vector
    def _(vector):
        vector.memset(onesf[:], 1.0)
        vector.wait_ge(CF, 64)
        vector.reduce_max(out=qn1[:], in_=qn8[:], axis=AX)
        vector.reduce_max(out=kn1[:], in_=kn8[:], axis=AX)
        vector.tensor_tensor(out=ch1[:], in0=qn1[:], in1=kn1[:],
                             op=ALU.mult).then_inc(CF, 1)

    @blk.scalar
    def _(scalar):
        scalar.wait_ge(CF, 65)
        scalar.activation(out=ch1b[:], in_=ch1[:], func=AF.Sqrt,
                          scale=0.0625).then_inc(CF, 1)

    @blk.sync
    def _(sync):
        sync.wait_ge(CF, 66)
        sync.dma_start(out=bass.AP(chb, 0, [[1, 8]]), in_=ch1b[:]).then_inc(CF, 16)
        sync.wait_ge(CF, 82)
        sync.dma_start(out=ch8[:],
                       in_=bass.AP(chb, 0, [[8, 1], [1, 8]])).then_inc(CF, 16)

    @blk.tensor
    def _(tensor):
        tensor.wait_ge(CF, 98)
        tensor.matmul(psCH[:, 0:8], onesf[:], ch8[:], start=True,
                      stop=True).then_inc(CF, 1)

    @blk.vector
    def _(vector):
        vector.wait_ge(CF, 99)
        vector.tensor_copy(out=chb_sb[:], in_=psCH[:, 0:8]).then_inc(CF, 1)

    @blk.sync
    def _(sync):
        sync.wait_ge(CF, 100)
        sync.dma_start(out=o_ch, in_=chb_sb[:]).then_inc(CF, 16)
        sync.wait_ge(CF, 116)
        sync.dma_start(out=o_kn, in_=kn1[:]).then_inc(CF, 16)
        sync.wait_ge(CF, 132)

    p.pop_scope()
    return p.done()



def _pad_ext(xb):
    xp = np.pad(xb, ((12, EXT), (0, 0)))
    return [np.ascontiguousarray(xp[SHIFT * j: SHIFT * j + EXT]) for j in range(4)]


_IDENTB = np.eye(128, dtype=ml_dtypes.bfloat16)
_IDENTF = np.eye(128, dtype=np.float32)


def a1_in_maps(inputs):
    x = np.asarray(inputs["L0_attn_x"], np.float32)
    bq = np.asarray(inputs["L0_attn_bq"], np.float32)
    bk = np.asarray(inputs["L0_attn_bk"], np.float32)
    bv = np.asarray(inputs["L0_attn_bv"], np.float32)
    maps = []
    for c in range(8):
        b, j = divmod(c, 4)
        maps.append(dict(
            x_full=np.ascontiguousarray(x[b]),
            x_ext=_pad_ext(x[b])[j],
            wq=np.asarray(inputs["L0_attn_Wq"], np.float32),
            wk=np.asarray(inputs["L0_attn_Wk"], np.float32),
            wv=np.asarray(inputs["L0_attn_Wv"], np.float32),
            bq_b=np.ascontiguousarray(np.broadcast_to(bq[None, :], (128, 128))),
            bk_b=np.ascontiguousarray(np.broadcast_to(bk[None, :], (128, 128))),
            bv_b=np.ascontiguousarray(np.broadcast_to(bv[None, :], (128, DM))),
            identb=_IDENTB,
        ))
    return maps





QCW = [512, 512, 512, 512, 16]
QCO = [0, 512, 1024, 1536, 2048]
NKT0 = 64
NG = H0 * len(QCW)


def build_a2():
    p = Prog()
    qp = p.din("qp", [16, H0 * QEXT], BF16)
    kp = p.din("kp", [16, H0 * S], BF16)
    vp = p.din("vp", [S, DM], BF16)
    anorm = p.din("anorm", [EXT, 8])
    kn = p.din("kn", [8, 1])
    xn_ext = p.din("xn_ext", [EXT, DM])
    gamma_b = p.din("gamma_b", [128, 1])
    identb = p.din("identb", [128, 128], BF16)

    o_s0 = p.dout("o_s0", [EXT, DM])
    att_dram = p.dint("att_dram", [EXT, DM])

    identb_sb = p.sb("identb_sb", [128, 128], BF16)
    onescol = p.sb("onescol", [128, 1])
    aqF = p.sb("aqF", [128, 8 * 17])
    kh = [p.sb(f"kh{i}", [16, S], BF16) for i in range(2)]
    qh = [p.sb(f"qh{i}", [16, QEXT], BF16) for i in range(2)]
    vh = [p.sb(f"vh{i}", [128, S], BF16) for i in range(2)]

    LW = p.sem("LW")

    p.push_scope()
    blk = p.block()
    kn_sb = p.sb("kn_sb", [8, 1])
    kfac = p.sb("kfac", [8, 1])
    kf1 = p.sb("kf1", [1, 8])
    kfb = p.dint("kfb", [8])
    kf8 = p.sb("kf8", [128, 8])
    ones1r = p.sb("ones1r", [1, 128])
    an_sb = p.sb("an_sb", [128, 8 * 17])
    sqa = p.sb("sqa", [128, 8 * 17])
    psB0 = p.ps("psB0", [128, 8], F32)
    ch0 = Chain2(p)
    ch0.dma("sync", lambda e: e.dma_start(out=identb_sb[:], in_=identb))
    ch0.dma("sync", lambda e: e.dma_start(out=kn_sb[:], in_=kn), writes=("kn",))
    ch0.add("vector", lambda e: e.memset(onescol[:], 1.0))
    ch0.add("vector", lambda e: e.memset(ones1r[:], 1.0), writes=("o1",))
    ch0.add("scalar", lambda e: e.activation(out=kfac[:], in_=kn_sb[:],
                                             func=AF.Sqrt, scale=1.0 / 16.0),
            reads=("kn",), writes=("kfac",))
    ch0.dma("sync", lambda e: e.dma_start(out=bass.AP(kfb, 0, [[1, 8]]),
                                          in_=kfac[:]),
            reads=("kfac",), writes=("kfb",))
    ch0.dma("sync", lambda e: e.dma_start(out=kf1[:],
                                          in_=bass.AP(kfb, 0, [[8, 1], [1, 8]])),
            reads=("kfb",), writes=("kf1",))
    ch0.add("tensor", lambda e: e.matmul(psB0[0:128, 0:8], ones1r[:], kf1[:],
                                         start=True, stop=True),
            reads=("o1", "kf1"), writes=("psB0",))
    ch0.add("vector", lambda e: e.tensor_copy(out=kf8[:], in_=psB0[0:128, 0:8]),
            reads=("psB0",), writes=("kf8",))
    ch0.dma("sync", lambda e: e.dma_start(
        out=an_sb[:].rearrange("p (a b) -> p a b", b=8),
        in_=bass.AP(anorm.tensor, 0, [[8, 128], [1024, 17], [1, 8]])),
        writes=("an",))
    ch0.add("scalar", lambda e: e.activation(out=sqa[:], in_=an_sb[:],
                                             func=AF.Sqrt),
            reads=("an",), writes=("sqa",))
    ch0.add("vector", lambda e: e.scalar_tensor_tensor(
        out=aqF[:].rearrange("p (a b) -> p a b", b=8),
        in0=sqa[:].rearrange("p (a b) -> p a b", b=8), scalar=-1.0,
        in1=kf8[:].rearrange("p (a b) -> p a b", b=8).broadcast_to([128, 17, 8]),
        op0=ALU.mult, op1=ALU.mult),
        reads=("sqa", "kf8"), writes=("aq0",))
    ch0.add("vector", lambda e: e.tensor_scalar(out=aqF[:], in0=aqF[:],
                                                scalar1=60.0, scalar2=None,
                                                op0=ALU.add),
            reads=("aq0",), writes=("aq",))
    ch0.emit(blk)
    p.pop_scope()

    p.push_scope()
    blk = p.block()
    KC = 1024
    NKC = S // KC
    NTS = KC // 512
    NTT = KC // 128
    QBL = []
    _q0 = 0
    while _q0 < QEXT:
        QBL.append((_q0, min(128, QEXT - _q0)))
        _q0 += 128
    NQB = len(QBL)
    NCH = NQB * NKC

    u2_sb = [p.sb(f"u2{i}", [128, KC]) for i in range(2)]
    b_sb = [p.sb(f"b{i}", [128, KC], BF16) for i in range(2)]
    bt_sb = [p.sb(f"bt{i}", [128, KC], BF16) for i in range(2)]
    dpart = p.sb("dpart", [128, NKC])
    den4 = p.sb("den4", [128, 4])
    rden = p.sb("rden", [128, 1])
    att_f = [p.sb(f"attf{i}", [128, 128]) for i in range(2)]
    alpha_sb = p.sb("a2_alpha", [128, 1])
    psS = [p.ps(f"psS{i}", [128, KC], F32) for i in range(3)]
    psT2 = p.ps("psT2", [128, KC], BF16)
    psO = p.ps("psO", [128, 128], F32)

    LH = [p.sem(f"LH{i}") for i in range(2)]
    TS = p.sem("TS")
    APL = p.sem("APL")
    AEX = p.sem("AEX")
    TTR = p.sem("TTR")
    EV = p.sem("EV")
    TA = p.sem("TA")
    VT = p.sem("VT")
    VD = p.sem("VD")
    SO = p.sem("SO")
    AL0 = p.sem("AL0")

    @blk.sync
    def _(sync):
        for h in range(H0):
            if h >= 2:
                sync.wait_ge(TA, NTT * NKC * NQB * (h - 1))
            sync.dma_start(out=kh[h % 2][:],
                           in_=kp[:, S * h:S * (h + 1)]).then_inc(LH[h % 2], 16)
            sync.wait_ge(LH[h % 2], 48 * (h // 2) + 16)
            sync.dma_start(out=qh[h % 2][:],
                           in_=qp[:, QEXT * h:QEXT * (h + 1)]).then_inc(LH[h % 2], 16)
            sync.wait_ge(LH[h % 2], 48 * (h // 2) + 32)
            sync.dma_start(
                out=vh[h % 2][:].rearrange("p (kt d) -> p kt d", d=128),
                in_=bass.AP(vp.tensor, 128 * h,
                            [[DM, 128], [128 * DM, 64], [1, 128]]),
            ).then_inc(LH[h % 2], 16)

    @blk.gpsimd
    def _(g_):
        for h in range(H0):
            for qb, (q0, qn) in enumerate(QBL):
                hq = h * NQB + qb
                g_.wait_ge(VT, hq + 1)
                g_.dma_start(out=att_dram.ap()[q0:q0 + qn, 128 * h:128 * (h + 1)],
                             in_=att_f[hq % 2][0:qn, :]).then_inc(SO, 16)
        g_.wait_ge(SO, 16 * H0 * NQB)

    @blk.tensor
    def _(tensor):
        def emit_transp(tensor, h, qb, qn, kc):
            m = (h * NQB + qb) * NKC + kc
            tensor.wait_ge(AEX, m + 1)
            if m >= 1:
                tensor.wait_ge(EV, m)
            for j in range(NTT):
                tensor.transpose(psT2[:, 128 * j:128 * j + qn],
                                 b_sb[m % 2][0:qn, 128 * j:128 * (j + 1)],
                                 identb_sb[0:qn, 0:qn]).then_inc(TTR, 1)

        def emit_av(tensor, h, qb, qn, kc):
            hq = h * NQB + qb
            m = hq * NKC + kc
            tensor.wait_ge(EV, m + 1)
            if kc == 0:
                tensor.wait_ge(VT, hq)
            for j in range(NTT):
                kt = NTT * kc + j
                tensor.matmul(psO[0:qn, 0:128],
                              bt_sb[m % 2][:, 128 * j:128 * j + qn],
                              vh[h % 2][:, 128 * kt:128 * (kt + 1)],
                              start=(kc == 0 and j == 0),
                              stop=(kc == NKC - 1 and j == NTT - 1)).then_inc(TA, 1)

        for h in range(H0):
            tensor.wait_ge(LH[h % 2], 48 * (h // 2 + 1))
            for qb, (q0, qn) in enumerate(QBL):
                hq = h * NQB + qb
                m0 = hq * NKC
                for kc in range(NKC):
                    m = m0 + kc
                    tensor.wait_ge(APL, max(0, m - 2))
                    for ts in range(NTS):
                        tensor.matmul(
                            psS[m % 3][0:qn, 512 * ts:512 * (ts + 1)],
                            qh[h % 2][:, q0:q0 + qn],
                            kh[h % 2][:, KC * kc + 512 * ts:KC * kc + 512 * (ts + 1)],
                            start=True, stop=True).then_inc(TS, 1)
                    if kc >= 1:
                        emit_transp(tensor, h, qb, qn, kc - 1)
                    if kc >= 2:
                        emit_av(tensor, h, qb, qn, kc - 2)
                emit_transp(tensor, h, qb, qn, NKC - 1)
                emit_av(tensor, h, qb, qn, NKC - 2)
                emit_av(tensor, h, qb, qn, NKC - 1)

    @blk.scalar
    def _(scalar):
        for h in range(H0):
            for qb, (q0, qn) in enumerate(QBL):
                hq = h * NQB + qb
                m0 = hq * NKC

                acol = 8 * qb + h

                def emit_exp(kc):
                    m = m0 + kc
                    scalar.wait_ge(APL, m + 1)
                    scalar.wait_ge(TTR, NTT * max(0, m - 1))
                    if kc == 0:
                        scalar.wait_ge(VT, hq)
                    scalar.activation(out=b_sb[m % 2][0:qn, :],
                                      in_=u2_sb[m % 2][0:qn, :], func=AF.Exp,
                                      scale=0.25, bias=aqF[0:qn, acol:acol + 1],
                                      accum_out=dpart[0:qn, kc:kc + 1]
                                      ).then_inc(AEX, 1)

                for kc in range(NKC):
                    m = m0 + kc
                    if m == 0:
                        scalar.wait_ge(AL0, 1)
                    scalar.wait_ge(TS, NTS * (m + 1))
                    scalar.wait_ge(AEX, max(0, m - 1))
                    if A2_SIM_RELU:
                        scalar.activation(out=u2_sb[m % 2][0:qn, :],
                                          in_=psS[m % 3][0:qn, :],
                                          func=AF.Relu).then_inc(APL, 1)
                    else:
                        scalar.activation(out=u2_sb[m % 2][0:qn, :],
                                          in_=psS[m % 3][0:qn, :], func=AF.Prelu,
                                          alpha=alpha_sb[0:qn, :]).then_inc(APL, 1)
                    if kc >= 1:
                        emit_exp(kc - 1)
                emit_exp(NKC - 1)

    @blk.vector
    def _(vector):
        vector.memset(alpha_sb[:], 0.2).then_inc(AL0, 1)
        for h in range(H0):
            for qb, (q0, qn) in enumerate(QBL):
                hq = h * NQB + qb
                m0 = hq * NKC
                vd0 = 5 * hq
                for kc in range(NKC):
                    m = m0 + kc
                    vector.wait_ge(TTR, NTT * (m + 1))
                    if m >= 2:
                        vector.wait_ge(TA, NTT * (m - 1))
                    vector.tensor_copy(out=bt_sb[m % 2][:],
                                       in_=psT2[:]).then_inc(EV, 1)
                vector.wait_ge(AEX, m0 + NKC)
                vector.tensor_tensor(out=den4[0:qn, :], in0=dpart[0:qn, 0:4],
                                     in1=dpart[0:qn, 4:8],
                                     op=ALU.add).then_inc(VD, 1)
                vector.wait_ge(VD, vd0 + 1)
                vector.tensor_tensor(out=den4[0:qn, 0:2], in0=den4[0:qn, 0:2],
                                     in1=den4[0:qn, 2:4],
                                     op=ALU.add).then_inc(VD, 1)
                vector.wait_ge(VD, vd0 + 2)
                vector.tensor_tensor(out=den4[0:qn, 0:1], in0=den4[0:qn, 0:1],
                                     in1=den4[0:qn, 1:2],
                                     op=ALU.add).then_inc(VD, 1)
                vector.wait_ge(VD, vd0 + 3)
                vector.reciprocal(out=rden[0:qn, :],
                                  in_=den4[0:qn, 0:1]).then_inc(VD, 1)
                vector.wait_ge(VD, vd0 + 4)
                vector.wait_ge(TA, NTT * NKC * (hq + 1))
                if hq >= 2:
                    vector.wait_ge(SO, 16 * (hq - 1))
                vector.tensor_scalar(out=att_f[hq % 2][0:qn, :],
                                     in0=psO[0:qn, 0:128],
                                     scalar1=rden[0:qn, :], scalar2=None,
                                     op0=ALU.mult).then_inc(VD, 1)
                vector.wait_ge(VD, vd0 + 5)
                vector.sem_inc(VT, 1)

    p.pop_scope()

    p.push_scope()
    blk = p.block()
    LX2 = [p.sem(f"LX2{i}") for i in range(2)]
    VE2 = p.sem("VE2")
    SO2 = p.sem("SO2")
    xn_t = [p.sb(f"a2xn{i}", [128, DM]) for i in range(2)]
    at_t = [p.sb(f"a2at{i}", [128, DM]) for i in range(2)]
    s0_t = [p.sb(f"a2s0{i}", [128, DM]) for i in range(2)]
    gam_sb = p.sb("gam_sb", [128, 1])
    VZ2 = p.sem("VZ2")

    @blk.vector
    def _(vector):
        vector.memset(at_t[0][:], 0.0)
        vector.memset(at_t[1][:], 0.0).then_inc(VZ2, 1)

    @blk.sync
    def _(sync):
        sync.wait_ge(VZ2, 1)
        sync.dma_start(out=gam_sb[:], in_=gamma_b).then_inc(LX2[0], 16)
        for t in range(NT):
            sync.wait_ge(VE2, max(0, t - 1))
            sync.dma_start(out=xn_t[t % 2][:],
                           in_=xn_ext[128 * t:128 * (t + 1), :]).then_inc(LX2[t % 2], 16)
            rows = QEXT - 128 * t if t == NT - 1 else 128
            sync.wait_ge(LX2[t % 2], 32 * (t // 2) + (32 if t % 2 == 0 else 16))
            sync.dma_start(out=at_t[t % 2][0:rows, :],
                           in_=att_dram.ap()[128 * t:128 * t + rows, :]
                           ).then_inc(LX2[t % 2], 16)
            if t >= 2:
                sync.wait_ge(VE2, t - 1)
                sync.dma_start(out=o_s0[128 * (t - 2):128 * (t - 1), :],
                               in_=s0_t[t % 2][:]).then_inc(SO2, 16)
        for t in (NT - 2, NT - 1):
            sync.wait_ge(VE2, t + 1)
            sync.dma_start(out=o_s0[128 * t:128 * (t + 1), :],
                           in_=s0_t[t % 2][:]).then_inc(SO2, 16)
        sync.wait_ge(SO2, 16 * NT)

    @blk.vector
    def _(vector):
        for t in range(NT):
            vector.wait_ge(LX2[t % 2], 32 * (t // 2 + 1) + (16 if t % 2 == 0 else 0))
            if t >= 2:
                vector.wait_ge(SO2, 16 * (t - 1))
            vector.scalar_tensor_tensor(out=s0_t[t % 2][:],
                                        in0=at_t[t % 2][:],
                                        scalar=gam_sb[:], in1=xn_t[t % 2][:],
                                        op0=ALU.mult, op1=ALU.add).then_inc(VE2, 1)

    p.pop_scope()
    return p.done()


def _a2_tt_before(g):
    tot = 0
    for gg in range(g):
        tot += (QCW[gg % 5] + 127) // 128
    return tot


def a2_in_maps(inputs, a1_outs):
    gamma = np.asarray(inputs["L0_attn_gamma"], np.float32).reshape(-1)
    maps = []
    for c in range(8):
        maps.append(dict(
            qp=a1_outs[c]["o_qp"],
            kp=a1_outs[c]["o_kp"],
            vp=a1_outs[c]["o_vp"],
            anorm=a1_outs[c]["o_anorm"],
            kn=a1_outs[c]["o_kn"],
            xn_ext=a1_outs[c]["o_xn"],
            gamma_b=np.ascontiguousarray(np.broadcast_to(gamma, (128, 1))),
            identb=_IDENTB,
        ))
    return maps





MAG = 12582912.0


def emit_quant(ch, p, xin, xdb, scr, cst):
    aabs, amax, scl, rscl, y, tb, q1, q2, q3, ay, m, sel1, qq = scr

    def v3(ap):
        return ap.rearrange("p (a b) -> p a b", b=32)

    ch.add("scalar", lambda e: e.activation(out=aabs[:], in_=xin[:], func=AF.Abs))
    ch.add("vector", lambda e: e.reduce_max(
        out=amax[:].rearrange("p (a b) -> p a b", b=1), in_=v3(aabs[:]), axis=AX))
    ch.add("vector", lambda e: e.tensor_scalar(
        out=scl[:], in0=amax[:], scalar1=1.0 / 7.5, scalar2=1e-30,
        op0=ALU.mult, op1=ALU.max))
    ch.add("vector", lambda e: e.reciprocal(out=rscl[:], in_=scl[:]))
    ch.add("vector", lambda e: e.tensor_tensor(
        out=v3(y[:]), in0=v3(xin[:]),
        in1=rscl[:].rearrange("p (a b) -> p a b", b=1).broadcast_to([128, 32, 32]),
        op=ALU.mult))
    for qi, (sc1, bc1, sc2, bc2) in enumerate(
            [(8.0, 0, 0.125, 1), (4.0, 0, 0.25, 2), (2.0, 0, 0.5, 3)]):
        qout = (q1, q2, q3)[qi]
        ch.add("scalar", lambda e, sc1=sc1, bc1=bc1: e.activation(
            out=tb[:], in_=y[:], func=AF.Identity, scale=sc1, bias=cst[:, bc1:bc1 + 1]))
        ch.add("scalar", lambda e, sc2=sc2, bc2=bc2, qout=qout: e.activation(
            out=qout[:], in_=tb[:], func=AF.Identity, scale=sc2,
            bias=cst[:, 4 + bc2 - 1:4 + bc2]))
    ch.add("vector", lambda e: e.tensor_tensor(
        out=v3(ay[:]), in0=v3(aabs[:]),
        in1=rscl[:].rearrange("p (a b) -> p a b", b=1).broadcast_to([128, 32, 32]),
        op=ALU.mult))
    ch.add("vector", lambda e: e.tensor_scalar(
        out=m[:], in0=ay[:], scalar1=4.0, scalar2=None, op0=ALU.is_lt))
    ch.add("vector", lambda e: e.select(out=sel1[:], mask=m[:], on_true=q2[:],
                                        on_false=q3[:]))
    ch.add("vector", lambda e: e.tensor_scalar(
        out=m[:], in0=ay[:], scalar1=2.0, scalar2=None, op0=ALU.is_lt))
    ch.add("vector", lambda e: e.select(out=qq[:], mask=m[:], on_true=q1[:],
                                        on_false=sel1[:]))
    ch.add("vector", lambda e: e.tensor_tensor(
        out=v3(xdb[:]), in0=v3(qq[:]),
        in1=scl[:].rearrange("p (a b) -> p a b", b=1).broadcast_to([128, 32, 32]),
        op=ALU.mult))


def emit_quant2(ch, p, xin, xdb, scr, cst, tg, rin, wout):
    aabs, amax, scl, rscl, y, tb, q1, q2, q3, ay, m, sel1, qq = scr

    def v3(ap):
        return ap.rearrange("p (a b) -> p a b", b=32)

    ch.add("scalar", lambda e: e.activation(out=aabs[:], in_=xin[:], func=AF.Abs),
           reads=(rin,), writes=(tg + "ab",))
    ch.add("vector", lambda e: e.reduce_max(
        out=amax[:].rearrange("p (a b) -> p a b", b=1), in_=v3(aabs[:]), axis=AX),
        reads=(tg + "ab",), writes=(tg + "am",))
    ch.add("vector", lambda e: e.tensor_scalar(
        out=scl[:], in0=amax[:], scalar1=1.0 / 7.5, scalar2=1e-30,
        op0=ALU.mult, op1=ALU.max),
        reads=(tg + "am",), writes=(tg + "sc",))
    ch.add("vector", lambda e: e.reciprocal(out=rscl[:], in_=scl[:]),
           reads=(tg + "sc",), writes=(tg + "rs",))
    ch.add("vector", lambda e: e.tensor_tensor(
        out=v3(y[:]), in0=v3(xin[:]),
        in1=rscl[:].rearrange("p (a b) -> p a b", b=1).broadcast_to([128, 32, 32]),
        op=ALU.mult),
        reads=(rin, tg + "rs"), writes=(tg + "y",))
    for qi, (sc1, bc1, sc2, bc2) in enumerate(
            [(8.0, 0, 0.125, 1), (4.0, 0, 0.25, 2), (2.0, 0, 0.5, 3)]):
        qout = (q1, q2, q3)[qi]
        ch.add("scalar", lambda e, sc1=sc1, bc1=bc1: e.activation(
            out=tb[:], in_=y[:], func=AF.Identity, scale=sc1,
            bias=cst[:, bc1:bc1 + 1]),
            reads=(tg + "y", "cst"), writes=(tg + "tb",))
        ch.add("scalar", lambda e, sc2=sc2, bc2=bc2, qout=qout: e.activation(
            out=qout[:], in_=tb[:], func=AF.Identity, scale=sc2,
            bias=cst[:, 4 + bc2 - 1:4 + bc2]),
            reads=(tg + "tb", "cst"), writes=(tg + f"q{qi}",))
    ch.add("vector", lambda e: e.tensor_tensor(
        out=v3(ay[:]), in0=v3(aabs[:]),
        in1=rscl[:].rearrange("p (a b) -> p a b", b=1).broadcast_to([128, 32, 32]),
        op=ALU.mult),
        reads=(tg + "ab", tg + "rs"), writes=(tg + "ay",))
    ch.add("vector", lambda e: e.tensor_scalar(
        out=m[:], in0=ay[:], scalar1=4.0, scalar2=None, op0=ALU.is_lt),
        reads=(tg + "ay",), writes=(tg + "m",))
    ch.add("vector", lambda e: e.select(out=sel1[:], mask=m[:], on_true=q2[:],
                                        on_false=q3[:]),
           reads=(tg + "m", tg + "q1", tg + "q2"), writes=(tg + "s1",))
    ch.add("vector", lambda e: e.tensor_scalar(
        out=m[:], in0=ay[:], scalar1=2.0, scalar2=None, op0=ALU.is_lt),
        reads=(tg + "ay",), writes=(tg + "m",))
    ch.add("vector", lambda e: e.select(out=qq[:], mask=m[:], on_true=q1[:],
                                        on_false=sel1[:]),
           reads=(tg + "m", tg + "q0", tg + "s1"), writes=(tg + "qq",))
    ch.add("vector", lambda e: e.tensor_tensor(
        out=v3(xdb[:]), in0=v3(qq[:]),
        in1=scl[:].rearrange("p (a b) -> p a b", b=1).broadcast_to([128, 32, 32]),
        op=ALU.mult),
        reads=(tg + "qq", tg + "sc"), writes=(wout,))


def build_b():
    p = Prog()
    xn_ext = p.din("xn_ext", [EXT, DM])
    x_ext = p.din("x_ext", [EXT, DM])
    s0_ext = p.din("s0_ext", [EXT, DM])
    w0 = p.din("w0", [DM, DM])
    b0_b = p.din("b0_b", [128, DM])
    wr = p.din("wr", [DM, 8])
    identb = p.din("identb", [128, 128], BF16)

    o_x1 = p.dout("o_x1", [EXT, DM])
    o_xn2 = p.dout("o_xn2", [EXT, DM])
    o_logits = p.dout("o_logits", [EXT, 8])
    xn2b_pad = p.dint("xn2b_pad", [EXT, DM], BF16)

    identb_sb = p.sb("identb_sb", [128, 128], BF16)
    wdT = p.sb("wdT", [128, 8 * DM], BF16)
    wr_sb = p.sb("wr_sb", [128, 64], BF16)
    b0_sb = p.sb("b0_sb", [128, DM])
    cst = p.sb("cst", [128, 8])

    p.push_scope()
    blk = p.block()
    scr = [[p.sb(f"q{i}_{n}", [128, DM if n not in ("amax", "scl", "rscl") else 32],
                 I32 if n == "m" else F32)
            for n in ("aabs", "amax", "scl", "rscl", "y", "tb", "q1", "q2", "q3",
                      "ay", "m", "sel1", "qq")] for i in range(2)]
    win = [p.sb(f"win{i}", [128, DM]) for i in range(2)]
    wdb = [p.sb(f"wdb{i}", [128, DM], BF16) for i in range(2)]
    psT0 = [p.ps(f"psT0_{i}", [128, 1024], BF16) for i in range(2)]
    ch = Chain2(p)
    ch.dma("sync", lambda e: e.dma_start(out=identb_sb[:], in_=identb),
           writes=("ident",))
    ch.dma("sync", lambda e: e.dma_start(out=b0_sb[:], in_=b0_b))
    wr_st = p.sb("wr_st", [128, 64])
    for c in range(8):
        ch.dma("sync", lambda e, c=c: e.dma_start(out=wr_st[:, 8 * c:8 * c + 8],
                                                  in_=wr[128 * c:128 * (c + 1), :]),
               writes=(f"wrst{c}",))
    ch.add("vector", lambda e: e.tensor_copy(out=wr_sb[:], in_=wr_st[:]),
           reads=tuple(f"wrst{c}" for c in range(8)), writes=("wr",))
    for col, val in [(0, MAG), (4, -MAG / 8), (5, -MAG / 4), (6, -MAG / 2)]:
        ch.add("vector", lambda e, col=col, val=val: e.memset(cst[:, col:col + 1],
                                                              val),
               writes=("cst",))
    for rc in range(8):
        q = rc % 2
        ch.dma("sync", lambda e, rc=rc, q=q: e.dma_start(
            out=win[q][:], in_=w0[128 * rc:128 * (rc + 1), :]),
            writes=(f"win{q}",))
        emit_quant2(ch, p, win[q], wdb[q], scr[q], cst, f"w{q}", f"win{q}",
                    f"wdb{q}")
        for cc in range(8):
            ch.add("tensor", lambda e, cc=cc, q=q: e.transpose(
                psT0[q][:, 128 * cc:128 * (cc + 1)],
                wdb[q][:, 128 * cc:128 * (cc + 1)], identb_sb[:]),
                reads=(f"wdb{q}", "ident"), writes=(f"psT0{q}",))
        for cc in range(8):
            ch.add("vector", lambda e, rc=rc, cc=cc, q=q: e.tensor_copy(
                out=wdT[:, 1024 * cc + 128 * rc:1024 * cc + 128 * (rc + 1)],
                in_=psT0[q][:, 128 * cc:128 * (cc + 1)]),
                reads=(f"psT0{q}",), writes=(f"wdT{rc}_{cc}",))
    ch.emit(blk)
    p.pop_scope()

    p.push_scope()
    blk = p.block()
    scr = [[p.sb(f"x{i}_{n}", [128, DM if n not in ("amax", "scl", "rscl") else 32],
                 I32 if n == "m" else F32)
            for n in ("aabs", "amax", "scl", "rscl", "y", "tb", "q1", "q2", "q3",
                      "ay", "m", "sel1", "qq")] for i in range(2)]
    xin = [p.sb(f"xin{i}", [128, DM]) for i in range(2)]
    xdb = [p.sb(f"xdb{i}", [128, DM], BF16) for i in range(2)]
    xfm = [p.sb(f"xfm{i}", [128, DM], BF16) for i in range(2)]
    ff = [p.sb(f"ff{i}", [128, DM]) for i in range(2)]
    gg = [p.sb(f"gg{i}", [128, DM]) for i in range(2)]
    xx = [p.sb(f"xx{i}", [128, DM]) for i in range(2)]
    ss = [p.sb(f"ss{i}", [128, DM]) for i in range(2)]
    x1t = [p.sb(f"x1t{i}", [128, DM]) for i in range(2)]
    psT1 = [p.ps(f"psT1_{i}", [128, 1024], BF16) for i in range(2)]
    psF = [p.ps(f"psF{i}", [128, 1024], F32) for i in range(2)]
    ch = Chain2(p)
    for t in range(NT):
        q = t % 2
        ch.dma("sync", lambda e, t=t, q=q: e.dma_start(
            out=xin[q][:], in_=xn_ext[128 * t:128 * (t + 1), :]),
            writes=(f"xin{q}",))
        emit_quant2(ch, p, xin[q], xdb[q], scr[q], cst, f"x{q}", f"xin{q}",
                    f"xdb{q}")
        for cc in range(8):
            ch.add("tensor", lambda e, cc=cc, q=q: e.transpose(
                psT1[q][:, 128 * cc:128 * (cc + 1)],
                xdb[q][:, 128 * cc:128 * (cc + 1)], identb_sb[:]),
                reads=(f"xdb{q}",), writes=(f"psT1{q}",))
        ch.add("vector", lambda e, q=q: e.tensor_copy(out=xfm[q][:],
                                                      in_=psT1[q][:]),
               reads=(f"psT1{q}",), writes=(f"xfm{q}",))
        for half in range(2):
            for fc in range(8):
                ch.add("tensor", lambda e, half=half, fc=fc, q=q: e.matmul(
                    psF[q][:, 512 * half:512 * (half + 1)],
                    xfm[q][:, 128 * fc:128 * (fc + 1)],
                    wdT[:, 1024 * fc + 512 * half:1024 * fc + 512 * (half + 1)],
                    start=(fc == 0), stop=(fc == 7)),
                    reads=(f"xfm{q}",), writes=(f"psF{q}",))
        ch.add("vector", lambda e, q=q: e.tensor_tensor(out=ff[q][:],
                                                        in0=psF[q][:],
                                                        in1=b0_sb[:], op=ALU.add),
               reads=(f"psF{q}",), writes=(f"ff{q}",))
        ch.add("scalar", lambda e, q=q: e.activation(out=gg[q][:], in_=ff[q][:],
                                                     func=AF.Silu),
               reads=(f"ff{q}",), writes=(f"gg{q}",))
        ch.dma("sync", lambda e, t=t, q=q: e.dma_start(
            out=xx[q][:], in_=x_ext[128 * t:128 * (t + 1), :]),
            writes=(f"xx{q}",))
        ch.dma("sync", lambda e, t=t, q=q: e.dma_start(
            out=ss[q][:], in_=s0_ext[128 * t:128 * (t + 1), :]),
            writes=(f"ss{q}",))
        ch.add("vector", lambda e, q=q: e.tensor_tensor(out=ss[q][:], in0=ss[q][:],
                                                        in1=gg[q][:], op=ALU.mult),
               reads=(f"ss{q}", f"gg{q}"), writes=(f"ss{q}",))
        ch.add("vector", lambda e, q=q: e.tensor_tensor(out=x1t[q][:], in0=xx[q][:],
                                                        in1=ss[q][:], op=ALU.add),
               reads=(f"xx{q}", f"ss{q}"), writes=(f"x1t{q}",))
        ch.dma("sync", lambda e, t=t, q=q: e.dma_start(
            out=o_x1[128 * t:128 * (t + 1), :], in_=x1t[q][:]),
            reads=(f"x1t{q}",), writes=(f"x1d{t}",))
    ch.emit(blk)
    p.pop_scope()

    p.push_scope()
    blk = p.block()
    emit_ln_stream(p, blk, "ln2", o_x1, NT, o_xn2, xn2b_pad.ap())
    p.pop_scope()

    p.push_scope()
    blk = p.block()
    xnb2 = [p.sb(f"xnb2_{i}", [128, DM], BF16) for i in range(2)]
    xfm2 = [p.sb(f"xfm2_{i}", [128, DM], BF16) for i in range(2)]
    lg = [p.sb(f"lg{i}", [128, 8]) for i in range(2)]
    psT2 = [p.ps(f"psT2_{i}", [128, 1024], BF16) for i in range(2)]
    psL = [p.ps(f"psL{i}", [128, 128], F32) for i in range(2)]
    ch = Chain2(p)
    for t in range(NT):
        q = t % 2
        ch.dma("sync", lambda e, t=t, q=q: e.dma_start(
            out=xnb2[q][:], in_=xn2b_pad.ap()[128 * t:128 * (t + 1), :]),
            writes=(f"xnb{q}",))
        for cc in range(8):
            ch.add("tensor", lambda e, cc=cc, q=q: e.transpose(
                psT2[q][:, 128 * cc:128 * (cc + 1)],
                xnb2[q][:, 128 * cc:128 * (cc + 1)], identb_sb[:]),
                reads=(f"xnb{q}",), writes=(f"psT2{q}",))
        ch.add("vector", lambda e, q=q: e.tensor_copy(out=xfm2[q][:],
                                                      in_=psT2[q][:]),
               reads=(f"psT2{q}",), writes=(f"xfm2{q}",))
        for fc in range(8):
            ch.add("tensor", lambda e, fc=fc, q=q: e.matmul(
                psL[q][:, 0:8], xfm2[q][:, 128 * fc:128 * (fc + 1)],
                wr_sb[:, 8 * fc:8 * (fc + 1)],
                start=(fc == 0), stop=(fc == 7)),
                reads=(f"xfm2{q}",), writes=(f"psL{q}",))
        ch.add("vector", lambda e, q=q: e.tensor_copy(out=lg[q][:],
                                                      in_=psL[q][:, 0:8]),
               reads=(f"psL{q}",), writes=(f"lg{q}",))
        ch.dma("sync", lambda e, t=t, q=q: e.dma_start(
            out=o_logits[128 * t:128 * (t + 1), :], in_=lg[q][:]),
            reads=(f"lg{q}",), writes=(f"lgd{t}",))
    ch.emit(blk)
    p.pop_scope()
    return p.done()


def b_in_maps(inputs, a1_outs, a2_outs):
    b0 = np.asarray(inputs["L0_ffn_bias"], np.float32)
    maps = []
    for c in range(8):
        b, j = divmod(c, 4)
        x = np.asarray(inputs["L0_attn_x"], np.float32)
        maps.append(dict(
            xn_ext=a1_outs[c]["o_xn"],
            x_ext=_pad_ext(x[b])[j],
            s0_ext=a2_outs[c]["o_s0"],
            w0=np.asarray(inputs["L0_ffn_weight"], np.float32),
            b0_b=np.ascontiguousarray(np.broadcast_to(b0, (128, DM))),
            wr=np.asarray(inputs["L1_ffn_Wr"], np.float32),
            identb=_IDENTB,
        ))
    return maps





Q_SCALE = 128.0 ** (-0.5) / math.log(2.0)
K_SCALE = math.log(1.0 + math.e) / math.log(2.0)
L1TILES = []
for _t in range(18):
    if _t == 0:
        L1TILES.append((0, 120, 0, 132, 1))
    elif _t < 17:
        L1TILES.append((120 * _t, 120, 120 * _t - 12, 132, 0))
    else:
        L1TILES.append((2040, 24, 2028, 36, 0))


def build_c():
    p = Prog()
    xn2_ext = p.din("xn2_ext", [EXT, DM])
    w1q = p.din("w1q", [DM, DM])
    w1k = p.din("w1k", [DM, DM])
    w1v = p.din("w1v", [DM, DM])
    wrel = p.din("wrel", [DM, DM])
    wpost = p.din("wpost", [DM, DM])
    pos_emb = p.din("pos_emb", [25, DM])
    pds_row = p.din("pds_row", [1, 128])
    identb = p.din("identb", [128, 128], BF16)
    mask_main = p.din("mask_main", [120, 132])
    mask_t0 = p.din("mask_t0", [120, 132])

    o_s1 = p.dout("o_s1", [EXT, DM])
    v1_dram = p.dint("v1_dram", [EXT, DM], BF16)
    rfl_dram = p.dint("rfl_dram", [6144])

    identb_sb = p.sb("identb_sb", [128, 128], BF16)
    q1_sb = p.sb("q1_sb", [128, H0 * QEXT], BF16)
    k1_sb = p.sb("k1_sb", [128, H0 * QEXT], BF16)
    att1fm = p.sb("att1fm", [128, H0 * EXT], BF16)
    relk_sb = p.sb("relk_sb", [128, H0 * 25], BF16)
    qs_col = p.sb("qs_col", [128, 1])
    mm_sb = p.sb("mm_sb", [120, 132])
    m0_sb = p.sb("m0_sb", [120, 132])

    w1qT = p.sb("w1qT", [128, 8 * DM], BF16)
    w1kT = p.sb("w1kT", [128, 8 * DM], BF16)
    w1vT = p.sb("w1vT", [128, 8 * DM], BF16)
    wpostT = p.sb("wpostT", [128, 8 * DM], BF16)
    wrelT = p.sb("wrelT", [128, 8 * DM], BF16)
    posT = p.sb("posT", [25, DM])
    posTb = p.sb("posTb", [128, 8 * 25], BF16)

    p.push_scope()
    blk = p.block()
    wstage = [p.sb(f"wstage{i}", [128, DM]) for i in range(2)]
    wbf = [p.sb(f"wbf{i}", [128, DM], BF16) for i in range(2)]
    sp_a = p.sb("sp_a", [1, 128])
    sp_e = p.sb("sp_e", [1, 128])
    sp_l = p.sb("sp_l", [1, 128])
    sp_r = p.sb("sp_r", [1, 128])
    one1 = p.sb("one1", [1, 1])
    oneb = p.sb("oneb", [1, 1], BF16)
    psC = [p.ps(f"psC{i}", [128, 1024], BF16) for i in range(2)]
    ch = Chain2(p)
    ch.dma("sync", lambda e: e.dma_start(out=identb_sb[:], in_=identb),
           writes=("ident",))
    ch.dma("sync", lambda e: e.dma_start(out=mm_sb[:], in_=mask_main))
    ch.dma("sync", lambda e: e.dma_start(out=m0_sb[:], in_=mask_t0))
    ch.dma("sync", lambda e: e.dma_start(out=posT[:], in_=pos_emb))
    ch.dma("sync", lambda e: e.dma_start(out=sp_a[:], in_=pds_row))
    ch.add("vector", lambda e: e.memset(one1[:], 1.0))
    ch.add("vector", lambda e: e.memset(oneb[:], 1.0))
    i = 0
    for wap, wT in [(w1q, w1qT), (w1k, w1kT), (w1v, w1vT), (wpost, wpostT),
                    (wrel, wrelT)]:
        for rc in range(8):
            q = i % 2
            i += 1
            ch.dma("sync", lambda e, wap=wap, rc=rc, q=q: e.dma_start(
                out=wstage[q][:], in_=wap[128 * rc:128 * (rc + 1), :]),
                writes=(f"ws{q}",))
            ch.add("vector", lambda e, q=q: e.tensor_copy(out=wbf[q][:],
                                                          in_=wstage[q][:]),
                   reads=(f"ws{q}",), writes=(f"wbf{q}",))
            for cc in range(8):
                ch.add("tensor", lambda e, cc=cc, q=q: e.transpose(
                    psC[q][:, 128 * cc:128 * (cc + 1)],
                    wbf[q][:, 128 * cc:128 * (cc + 1)], identb_sb[:]),
                    reads=(f"wbf{q}", "ident"), writes=(f"psC{q}",))
            for cc in range(8):
                ch.add("vector", lambda e, wT=wT, rc=rc, cc=cc, q=q: e.tensor_copy(
                    out=wT[:, 1024 * cc + 128 * rc:1024 * cc + 128 * (rc + 1)],
                    in_=psC[q][:, 128 * cc:128 * (cc + 1)]),
                    reads=(f"psC{q}",), writes=(f"wslice{id(wT)}_{rc}_{cc}",))
    ch.emit(blk)
    p.pop_scope()

    p.push_scope()
    blk = p.block()
    posb = p.sb("posb", [25, DM], BF16)
    psP2 = p.ps("psP2", [128, 1024], BF16)
    psRK2 = p.ps("psRK2", [128, 512], F32)
    psQ1 = p.ps("psQ1", [128, 512], F32)
    ch = Chain(p)
    ch.add("vector", lambda e: e.tensor_copy(out=posb[:], in_=posT[:]))
    for cc in range(8):
        ch.add("tensor", lambda e, cc=cc: e.transpose(
            psP2[:, 0:25], posb[:, 128 * cc:128 * (cc + 1)], identb_sb[0:25, 0:25]))
        ch.add("vector", lambda e, cc=cc: e.tensor_copy(
            out=posTb[:, 25 * cc:25 * (cc + 1)], in_=psP2[:, 0:25]))
    for h in range(H0):
        for fc in range(8):
            ch.add("tensor", lambda e, h=h, fc=fc: e.matmul(
                psRK2[:, 0:25],
                wrelT[:, 1024 * fc + 128 * h:1024 * fc + 128 * (h + 1)],
                posTb[:, 25 * fc:25 * (fc + 1)],
                start=(fc == 0), stop=(fc == 7)))
        ch.add("vector", lambda e, h=h: e.tensor_copy(
            out=relk_sb[:, 25 * h:25 * (h + 1)], in_=psRK2[:, 0:25]))
    ch.add("scalar", lambda e: e.activation(out=sp_e[:], in_=sp_a[:], func=AF.Abs))
    ch.add("scalar", lambda e: e.activation(out=sp_e[:], in_=sp_e[:], func=AF.Exp,
                                            scale=-1.0))
    ch.add("scalar", lambda e: e.activation(out=sp_l[:], in_=sp_e[:], func=AF.Ln,
                                            bias=one1[:]))
    ch.add("scalar", lambda e: e.activation(out=sp_r[:], in_=sp_a[:], func=AF.Relu))
    ch.add("vector", lambda e: e.tensor_tensor(out=sp_l[:], in0=sp_l[:], in1=sp_r[:],
                                               op=ALU.add))
    ch.add("vector", lambda e: e.tensor_scalar(out=sp_l[:], in0=sp_l[:],
                                               scalar1=Q_SCALE, scalar2=None,
                                               op0=ALU.mult))
    ch.add("tensor", lambda e: e.matmul(psQ1[0:128, 0:1], sp_l[:], one1[:],
                                        start=True, stop=True))
    ch.add("vector", lambda e: e.tensor_copy(out=qs_col[:], in_=psQ1[0:128, 0:1]))
    ch.emit(blk)
    p.pop_scope()

    p.push_scope()
    blk = p.block()
    xin = [p.sb(f"c1xin{i}", [128, DM]) for i in range(2)]
    xb = [p.sb(f"c1xb{i}", [128, DM], BF16) for i in range(2)]
    xfm = [p.sb(f"c1xfm{i}", [128, DM], BF16) for i in range(2)]
    vout = [p.sb(f"c1v{i}", [128, DM], BF16) for i in range(2)]
    psT1 = [p.ps(f"c1psT{i}", [128, 1024], BF16) for i in range(2)]
    psQ = [p.ps(f"c1psQ{i}", [128, 128], F32) for i in range(2)]
    psV = [p.ps(f"c1psV{i}", [128, 512], F32) for i in range(2)]
    ch = Chain2(p)
    for t in range(NT):
        q = t % 2
        ch.dma("sync", lambda e, t=t, q=q: e.dma_start(
            out=xin[q][:], in_=xn2_ext[128 * t:128 * (t + 1), :]),
            writes=(f"xin{q}",))
        ch.add("vector", lambda e, q=q: e.tensor_copy(out=xb[q][:], in_=xin[q][:]),
               reads=(f"xin{q}",), writes=(f"xb{q}",))
        for cc in range(8):
            ch.add("tensor", lambda e, cc=cc, q=q: e.transpose(
                psT1[q][:, 128 * cc:128 * (cc + 1)],
                xb[q][:, 128 * cc:128 * (cc + 1)], identb_sb[:]),
                reads=(f"xb{q}",), writes=(f"psT{q}",))
        ch.add("vector", lambda e, q=q: e.tensor_copy(out=xfm[q][:], in_=psT1[q][:]),
               reads=(f"psT{q}",), writes=(f"xfm{q}",))
        w = min(128, QEXT - 128 * t)
        if w > 0:
            for h in range(H0):
                hq = h % 2
                for fc in range(8):
                    ch.add("tensor", lambda e, h=h, fc=fc, q=q, hq=hq: e.matmul(
                        psQ[hq][:, 0:128],
                        w1qT[:, 1024 * fc + 128 * h:1024 * fc + 128 * (h + 1)],
                        xfm[q][:, 128 * fc:128 * (fc + 1)],
                        start=(fc == 0), stop=(fc == 7)),
                        reads=(f"xfm{q}",), writes=(f"psQ{hq}",))
                ch.add("vector", lambda e, h=h, t=t, w=w, hq=hq: e.tensor_scalar(
                    out=q1_sb[:, QEXT * h + 128 * t:QEXT * h + 128 * t + w],
                    in0=psQ[hq][:, 0:w], scalar1=qs_col[:], scalar2=None,
                    op0=ALU.mult),
                    reads=(f"psQ{hq}", "qs"), writes=(f"q1_{t}_{h}",))
                for fc in range(8):
                    ch.add("tensor", lambda e, h=h, fc=fc, q=q, hq=hq: e.matmul(
                        psQ[hq][:, 0:128],
                        w1kT[:, 1024 * fc + 128 * h:1024 * fc + 128 * (h + 1)],
                        xfm[q][:, 128 * fc:128 * (fc + 1)],
                        start=(fc == 0), stop=(fc == 7)),
                        reads=(f"xfm{q}",), writes=(f"psQ{hq}",))
                ch.add("vector", lambda e, h=h, t=t, w=w, hq=hq: e.tensor_scalar(
                    out=k1_sb[:, QEXT * h + 128 * t:QEXT * h + 128 * t + w],
                    in0=psQ[hq][:, 0:w], scalar1=K_SCALE, scalar2=None,
                    op0=ALU.mult),
                    reads=(f"psQ{hq}",), writes=(f"k1_{t}_{h}",))
            for half in range(2):
                for fc in range(8):
                    ch.add("tensor", lambda e, half=half, fc=fc, q=q: e.matmul(
                        psV[half][:, 0:512],
                        xfm[q][:, 128 * fc:128 * (fc + 1)],
                        w1vT[:, 1024 * fc + 512 * half:1024 * fc + 512 * (half + 1)],
                        start=(fc == 0), stop=(fc == 7)),
                        reads=(f"xfm{q}",), writes=(f"psV{half}",))
            for half in range(2):
                ch.add("vector", lambda e, half=half, q=q: e.tensor_copy(
                    out=vout[q][:, 512 * half:512 * (half + 1)],
                    in_=psV[half][:, 0:512]),
                    reads=(f"psV{half}",), writes=(f"vout{q}",))
            ch.dma("sync", lambda e, t=t, w=w, q=q: e.dma_start(
                out=v1_dram.ap()[128 * t:128 * t + w, :], in_=vout[q][0:w, :]),
                reads=(f"vout{q}",), writes=(f"v1d_{t}",))
    ch.emit(blk)
    p.pop_scope()

    p.push_scope()
    blk = p.block()
    rst = [p.sb(f"rst{i}", [120, 25]) for i in range(2)]
    bd_sb = [p.sb(f"bd_sb{i}", [120, 132]) for i in range(2)]
    lg_sb = [p.sb(f"lg_sb{i}", [120, 132]) for i in range(2)]
    th_sb = [p.sb(f"th_sb{i}", [120, 132]) for i in range(2)]
    uu_sb = [p.sb(f"uu_sb{i}", [120, 132]) for i in range(2)]
    mx_sb = [p.sb(f"mx_sb{i}", [120, 1]) for i in range(2)]
    nb_sb = [p.sb(f"nb_sb{i}", [120, 1]) for i in range(2)]
    den_sb = [p.sb(f"den_sb{i}", [120, 1]) for i in range(2)]
    rden_sb = [p.sb(f"rden_sb{i}", [120, 1]) for i in range(2)]
    bb_sb = [p.sb(f"bb_sb{i}", [120, 132], BF16) for i in range(2)]
    bT1 = [p.sb(f"bT1_{i}", [128, 120], BF16) for i in range(2)]
    bT2 = [p.sb(f"bT2_{i}", [4, 120], BF16) for i in range(2)]
    vslA = [p.sb(f"vslA{i}", [128, 128], BF16) for i in range(2)]
    vslB = [p.sb(f"vslB{i}", [4, 128], BF16) for i in range(2)]
    ao_sb = [p.sb(f"ao_sb{i}", [120, 128], BF16) for i in range(2)]
    psRb = [p.ps(f"c2psR{i}", [128, 128], F32) for i in range(2)]
    psA2b = [p.ps(f"c2psA2{i}", [128, 132], F32) for i in range(2)]
    psV2b = [p.ps(f"c2psV2{i}", [128, 128], F32) for i in range(2)]
    psB = [p.ps(f"c2psB{i}", [128, 512], BF16) for i in range(2)]
    ch = Chain2(p)
    for h in range(H0):
        ch.add("vector", lambda e, h=h: e.memset(
            att1fm[:, EXT * h + QEXT:EXT * (h + 1)], 0.0))
    i = 0
    for h in range(H0):
        for (q0, qn, slab0, slabw, mknd) in L1TILES:
            msk = m0_sb if mknd else mm_sb
            q = i % 2
            i += 1
            psR = psRb[q][:]
            psA2 = psA2b[q][:]
            psV2 = psV2b[q][:]
            psBT = psB[q][:, 0:128]
            psBT2 = psB[q][:, 128:256]
            psAT = psB[q][:, 256:384]
            sw1 = min(128, slabw)
            ch.add("tensor", lambda e, h=h, q0=q0, qn=qn, psR=psR: e.matmul(
                psR[0:qn, 0:25], q1_sb[:, QEXT * h + q0:QEXT * h + q0 + qn],
                relk_sb[:, 25 * h:25 * (h + 1)], start=True, stop=True),
                writes=(f"psR{q}",))
            ch.add("vector", lambda e, qn=qn, q=q, psR=psR: e.tensor_copy(
                out=rst[q][0:qn, :], in_=psR[0:qn, 0:25]),
                reads=(f"psR{q}",), writes=(f"rst{q}",))
            ch.dma("sync", lambda e, qn=qn, q=q: e.dma_start(
                out=bass.AP(rfl_dram, 2048 * q, [[25, qn], [1, 25]]),
                in_=rst[q][0:qn, :]),
                reads=(f"rst{q}",), writes=(f"rfl{q}",))
            ch.dma("sync", lambda e, qn=qn, slabw=slabw, mknd=mknd, q=q: e.dma_start(
                out=bd_sb[q][0:qn, 0:slabw],
                in_=bass.AP(rfl_dram, 2048 * q + (12 if mknd else 0),
                            [[24, qn], [1, slabw]])),
                reads=(f"rfl{q}",), writes=(f"bd{q}",))
            ch.add("tensor", lambda e, h=h, q0=q0, qn=qn, slab0=slab0, slabw=slabw,
                   psA2=psA2: e.matmul(
                       psA2[0:qn, 0:slabw],
                       q1_sb[:, QEXT * h + q0:QEXT * h + q0 + qn],
                       k1_sb[:, QEXT * h + slab0:QEXT * h + slab0 + slabw],
                       start=True, stop=True),
                   writes=(f"psA2{q}",))
            ch.add("vector", lambda e, qn=qn, slabw=slabw, q=q, psA2=psA2:
                   e.tensor_tensor(
                       out=lg_sb[q][0:qn, 0:slabw], in0=psA2[0:qn, 0:slabw],
                       in1=bd_sb[q][0:qn, 0:slabw], op=ALU.add),
                   reads=(f"psA2{q}", f"bd{q}"), writes=(f"lg{q}",))
            ch.add("scalar", lambda e, qn=qn, slabw=slabw, q=q: e.activation(
                out=th_sb[q][0:qn, 0:slabw], in_=lg_sb[q][0:qn, 0:slabw],
                func=AF.Tanh, scale=0.02),
                reads=(f"lg{q}",), writes=(f"th{q}",))
            ch.add("vector", lambda e, qn=qn, slabw=slabw, msk=msk, q=q:
                   e.tensor_tensor(
                       out=uu_sb[q][0:qn, 0:slabw], in0=th_sb[q][0:qn, 0:slabw],
                       in1=msk[0:qn, 0:slabw], op=ALU.add),
                   reads=(f"th{q}",), writes=(f"uu{q}",))
            ch.add("vector", lambda e, qn=qn, slabw=slabw, q=q: e.reduce_max(
                out=mx_sb[q][0:qn, :], in_=uu_sb[q][0:qn, 0:slabw], axis=AX),
                reads=(f"uu{q}",), writes=(f"mx{q}",))
            ch.add("vector", lambda e, qn=qn, q=q: e.tensor_scalar(
                out=nb_sb[q][0:qn, :], in0=mx_sb[q][0:qn, :], scalar1=-50.0,
                scalar2=None, op0=ALU.mult),
                reads=(f"mx{q}",), writes=(f"nb{q}",))
            ch.add("scalar", lambda e, qn=qn, slabw=slabw, q=q: e.activation(
                out=bb_sb[q][0:qn, 0:slabw], in_=uu_sb[q][0:qn, 0:slabw],
                func=AF.Exp, scale=50.0, bias=nb_sb[q][0:qn, :],
                accum_out=den_sb[q][0:qn, :]),
                reads=(f"uu{q}", f"nb{q}"), writes=(f"bb{q}", f"den{q}"))
            ch.add("vector", lambda e, qn=qn, q=q: e.reciprocal(
                out=rden_sb[q][0:qn, :], in_=den_sb[q][0:qn, :]),
                reads=(f"den{q}",), writes=(f"rden{q}",))
            ch.add("tensor", lambda e, qn=qn, sw1=sw1, q=q, psBT=psBT: e.transpose(
                psBT[0:sw1, 0:qn], bb_sb[q][0:qn, 0:sw1],
                identb_sb[0:qn, 0:qn]),
                reads=(f"bb{q}",), writes=(f"psBT{q}",))
            ch.add("vector", lambda e, qn=qn, sw1=sw1, q=q, psBT=psBT: e.tensor_copy(
                out=bT1[q][0:sw1, 0:qn], in_=psBT[0:sw1, 0:qn]),
                reads=(f"psBT{q}",), writes=(f"bT1{q}",))
            if slabw > 128:
                ch.add("tensor", lambda e, qn=qn, slabw=slabw, q=q, psBT2=psBT2:
                       e.transpose(
                           psBT2[0:slabw - 128, 0:qn], bb_sb[q][0:qn, 128:slabw],
                           identb_sb[0:qn, 0:qn]),
                       reads=(f"bb{q}",), writes=(f"psBT2{q}",))
                ch.add("vector", lambda e, qn=qn, slabw=slabw, q=q, psBT2=psBT2:
                       e.tensor_copy(
                           out=bT2[q][0:slabw - 128, 0:qn],
                           in_=psBT2[0:slabw - 128, 0:qn]),
                       reads=(f"psBT2{q}",), writes=(f"bT2{q}",))
            ch.dma("sync", lambda e, h=h, slab0=slab0, sw1=sw1, q=q: e.dma_start(
                out=vslA[q][0:sw1, :],
                in_=v1_dram.ap()[slab0:slab0 + sw1, 128 * h:128 * (h + 1)]),
                writes=(f"vslA{q}",))
            if slabw > 128:
                ch.dma("sync", lambda e, h=h, slab0=slab0, slabw=slabw, q=q:
                       e.dma_start(
                           out=vslB[q][0:slabw - 128, :],
                           in_=v1_dram.ap()[slab0 + 128:slab0 + slabw,
                                            128 * h:128 * (h + 1)]),
                       writes=(f"vslB{q}",))
            ch.add("tensor", lambda e, qn=qn, slabw=slabw, sw1=sw1, q=q, psV2=psV2:
                   e.matmul(
                       psV2[0:qn, 0:128], bT1[q][0:sw1, 0:qn],
                       vslA[q][0:sw1, :], start=True, stop=(slabw <= 128)),
                   reads=(f"bT1{q}", f"vslA{q}"), writes=(f"psV2{q}",))
            if slabw > 128:
                ch.add("tensor", lambda e, qn=qn, slabw=slabw, q=q, psV2=psV2:
                       e.matmul(
                           psV2[0:qn, 0:128], bT2[q][0:slabw - 128, 0:qn],
                           vslB[q][0:slabw - 128, :], start=False, stop=True),
                       reads=(f"bT2{q}", f"vslB{q}"), writes=(f"psV2{q}",))
            ch.add("vector", lambda e, qn=qn, q=q, psV2=psV2: e.tensor_scalar(
                out=ao_sb[q][0:qn, :], in0=psV2[0:qn, 0:128],
                scalar1=rden_sb[q][0:qn, :], scalar2=None, op0=ALU.mult),
                reads=(f"psV2{q}", f"rden{q}"), writes=(f"ao{q}",))
            ch.add("tensor", lambda e, qn=qn, q=q, psAT=psAT: e.transpose(
                psAT[0:128, 0:qn], ao_sb[q][0:qn, :], identb_sb[0:qn, 0:qn]),
                reads=(f"ao{q}",), writes=(f"psAT{q}",))
            ch.add("vector", lambda e, h=h, q0=q0, qn=qn, q=q, psAT=psAT:
                   e.tensor_copy(
                       out=att1fm[:, EXT * h + q0:EXT * h + q0 + qn],
                       in_=psAT[0:128, 0:qn]),
                   reads=(f"psAT{q}",), writes=(f"afm_{h}_{q0}",))
    ch.emit(blk)
    p.pop_scope()

    p.push_scope()
    blk = p.block()
    s1t = p.sb("s1t", [128, DM])
    psP = p.ps("psP", [128, 1024], F32)
    ch = Chain(p)
    for t in range(NT):
        for half in range(2):
            for fc in range(8):
                ch.add("tensor", lambda e, t=t, half=half, fc=fc: e.matmul(
                    psP[:, 512 * half:512 * (half + 1)],
                    att1fm[:, EXT * fc + 128 * t:EXT * fc + 128 * (t + 1)],
                    wpostT[:, 1024 * fc + 512 * half:1024 * fc + 512 * (half + 1)],
                    start=(fc == 0), stop=(fc == 7)))
        ch.add("vector", lambda e: e.tensor_copy(out=s1t[:], in_=psP[:]))
        ch.dma("sync", lambda e, t=t: e.dma_start(
            out=o_s1[128 * t:128 * (t + 1), :], in_=s1t[:]))
    ch.emit(blk)
    p.pop_scope()
    return p.done()


def _l1_masks():
    mm = np.full((120, 132), -2001.0, np.float32)
    m0 = np.full((120, 132), -2001.0, np.float32)
    for i in range(120):
        blk = i // 12
        lo, hi = 12 * blk, 12 * blk + 24
        mm[i, lo:hi] = 0.0
        lo0, hi0 = 12 * blk - 12, 12 * blk + 12
        m0[i, max(0, lo0):hi0] = 0.0
    return mm, m0


def c_in_maps(inputs, b_outs):
    pds = np.asarray(inputs["L1_attn_per_dim_scale"], np.float32)
    mm, m0 = _l1_masks()
    maps = []
    for c in range(8):
        maps.append(dict(
            xn2_ext=b_outs[c]["o_xn2"],
            w1q=np.asarray(inputs["L1_attn_Wq"], np.float32),
            w1k=np.asarray(inputs["L1_attn_Wk"], np.float32),
            w1v=np.asarray(inputs["L1_attn_Wv"], np.float32),
            wrel=np.asarray(inputs["L1_attn_Wrel"], np.float32),
            wpost=np.asarray(inputs["L1_attn_Wpost"], np.float32),
            pos_emb=np.asarray(inputs["L1_attn_position_embeddings"], np.float32),
            pds_row=np.ascontiguousarray(pds[None, :]),
            identb=_IDENTB,
            mask_main=mm,
            mask_t0=m0,
        ))
    return maps





def emit_routing(p, ch, lg_all, consts, out_sb, rt_dram=None, slot=None):
    nc = p.nc
    B = [p.sb(f"rt_B{i}", [128, 1024]) for i in range(6)]
    mx = p.sb("rt_mx", [128, 128])
    dd = B[0]
    ee = B[1]
    se = p.sb("rt_se", [128, 128])
    pmax = out_sb["pmax"]
    eq = B[2]
    cs = B[3]
    cs2 = B[4]
    t1 = B[5]
    oh = B[0]
    cums = B[1]
    idxf = p.sb("rt_idx", [128, 128])
    cnt_sb = p.sb("rt_cnt", [1, 1024])
    cnt128 = p.sb("rt_c128", [128, 8])
    offs_sb = p.sb("rt_offs", [128, 8])
    offl = p.sb("rt_offl", [1, 1024])
    offs_b = B[3]
    r1 = B[4]
    pos = p.sb("rt_pos", [128, 128])
    kept = out_sb["kept"]
    flat = out_sb["flatm"]
    fl0 = p.sb("rt_fl0", [128, 128])
    tA = p.sb("rt_tA", [128, 128])
    tB = p.sb("rt_tB", [128, 128])
    gid = p.ctx.enter_context(nc.sbuf_tensor("rt_gid", [128, 128], I32)) \
        if False else p.sb("rt_gid", [128, 128], I32)
    sidx = p.sb("rt_sidx", [128, 128], I32)
    keptI = p.sb("rt_keptI", [128, 128], I32)
    psC1 = p.ps("rt_psC1", [128, 1024], F32)
    psC2 = p.ps("rt_psC2", [128, 512], F32)
    psC3 = p.ps("rt_psC3", [128, 512], F32)
    bounce = p.dint(f"rt_bounce_{id(ch) % 997}", [2048])

    def v3(ap):
        return ap.rearrange("p (a b) -> p a b", b=8)

    def bc(ap):
        return ap.rearrange("p (a b) -> p a b", b=1).broadcast_to([128, 128, 8])

    C = consts
    ch.add("vector", lambda e: e.reduce_max(
        out=mx[:].rearrange("p (a b) -> p a b", b=1), in_=v3(lg_all[:]), axis=AX))
    ch.add("vector", lambda e: e.tensor_tensor(out=v3(dd[:]), in0=v3(lg_all[:]),
                                               in1=bc(mx[:]), op=ALU.subtract))
    ch.add("scalar", lambda e: e.activation(out=ee[:], in_=dd[:], func=AF.Exp))
    ch.add("vector", lambda e: e.reduce_sum(
        out=se[:].rearrange("p (a b) -> p a b", b=1), in_=v3(ee[:]), axis=AX))
    ch.add("vector", lambda e: e.reciprocal(out=pmax[:], in_=se[:]))
    ch.add("vector", lambda e: e.tensor_tensor(out=v3(eq[:]), in0=v3(lg_all[:]),
                                               in1=bc(mx[:]), op=ALU.is_equal))
    ch.add("vector", lambda e: e.tensor_copy(out=cs[:], in_=eq[:]))
    for k, (src, dst) in zip((1, 2, 4), ((cs, cs2), (cs2, cs), (cs, cs2))):
        ch.add("vector", lambda e, k=k, src=src, dst=dst: e.tensor_tensor(
            out=v3(dst[:])[:, :, k:8], in0=v3(src[:])[:, :, k:8],
            in1=v3(src[:])[:, :, 0:8 - k], op=ALU.add))
        ch.add("vector", lambda e, k=k, src=src, dst=dst: e.tensor_copy(
            out=v3(dst[:])[:, :, 0:k], in_=v3(src[:])[:, :, 0:k]))
    ch.add("vector", lambda e: e.tensor_scalar(out=t1[:], in0=cs2[:], scalar1=1.0,
                                               scalar2=None, op0=ALU.is_equal))
    ch.add("vector", lambda e: e.tensor_tensor(out=oh[:], in0=t1[:], in1=eq[:],
                                               op=ALU.mult))
    ch.add("vector", lambda e: e.tensor_tensor(out=v3(t1[:]), in0=v3(oh[:]),
                                               in1=C["iota8"][:].rearrange(
                                                   "p (a b) -> p a b", b=8
                                               ).broadcast_to([128, 128, 8]),
                                               op=ALU.mult))
    ch.add("vector", lambda e: e.reduce_sum(
        out=idxf[:].rearrange("p (a b) -> p a b", b=1), in_=v3(t1[:]), axis=AX))
    for half in range(2):
        ch.add("tensor", lambda e, half=half: e.matmul(
            psC1[:, 512 * half:512 * (half + 1)], C["triu_incl"][:],
            oh[:, 512 * half:512 * (half + 1)], start=True, stop=True))
    ch.add("vector", lambda e: e.tensor_copy(out=cums[:], in_=psC1[:]))
    for half in range(2):
        ch.add("tensor", lambda e, half=half: e.matmul(
            psC2[0:1, 0:512], C["onescol_f"][:],
            oh[:, 512 * half:512 * (half + 1)], start=True, stop=True))
        ch.add("vector", lambda e, half=half: e.tensor_copy(
            out=cnt_sb[:, 512 * half:512 * (half + 1)], in_=psC2[0:1, 0:512]))
    ch.dma("sync", lambda e: e.dma_start(out=bass.AP(bounce, 0, [[1, 1024]]),
                                         in_=cnt_sb[:]))
    ch.dma("sync", lambda e: e.dma_start(
        out=cnt128[:], in_=bass.AP(bounce, 0, [[8, 128], [1, 8]])))
    ch.add("tensor", lambda e: e.matmul(psC3[0:128, 0:8], C["trs"][:], cnt128[:],
                                        start=True, stop=True))
    ch.add("vector", lambda e: e.tensor_copy(out=offs_sb[:], in_=psC3[0:128, 0:8]))
    ch.dma("sync", lambda e: e.dma_start(out=bass.AP(bounce, 1024, [[8, 128], [1, 8]]),
                                         in_=offs_sb[:]))
    ch.dma("sync", lambda e: e.dma_start(out=offl[:],
                                         in_=bass.AP(bounce, 1024, [[1, 1024]])))
    for half in range(2):
        ch.add("tensor", lambda e, half=half: e.matmul(
            psC2[:, 0:512], C["onesrow_f"][:],
            offl[:, 512 * half:512 * (half + 1)], start=True, stop=True))
        ch.add("vector", lambda e, half=half: e.tensor_copy(
            out=offs_b[:, 512 * half:512 * (half + 1)], in_=psC2[:, 0:512]))
    ch.add("vector", lambda e: e.tensor_tensor(out=r1[:], in0=cums[:], in1=offs_b[:],
                                               op=ALU.add))
    ch.add("vector", lambda e: e.tensor_scalar(out=r1[:], in0=r1[:], scalar1=1.0,
                                               scalar2=None, op0=ALU.subtract))
    ch.add("vector", lambda e: e.tensor_tensor(out=r1[:], in0=r1[:], in1=oh[:],
                                               op=ALU.mult))
    ch.add("vector", lambda e: e.reduce_sum(
        out=pos[:].rearrange("p (a b) -> p a b", b=1), in_=v3(r1[:]), axis=AX))
    if "rank1" in out_sb:
        ch.add("vector", lambda e: e.tensor_tensor(out=t1[:], in0=cums[:],
                                                   in1=oh[:], op=ALU.mult))
        ch.add("vector", lambda e: e.reduce_sum(
            out=out_sb["rank1"][:].rearrange("p (a b) -> p a b", b=1),
            in_=v3(t1[:]), axis=AX))
    out_sb["_idxf"] = idxf
    out_sb["_cnt128"] = cnt128
    out_sb["_offs"] = offs_sb
    ch.add("vector", lambda e: e.tensor_scalar(out=kept[:], in0=pos[:],
                                               scalar1=float(CAP), scalar2=None,
                                               op0=ALU.is_lt))
    ch.add("vector", lambda e: e.tensor_scalar(out=tA[:], in0=idxf[:],
                                               scalar1=float(CAP), scalar2=None,
                                               op0=ALU.mult))
    ch.add("vector", lambda e: e.tensor_scalar(out=tB[:], in0=pos[:],
                                               scalar1=float(CAP - 1), scalar2=None,
                                               op0=ALU.min))
    ch.add("vector", lambda e: e.tensor_tensor(out=fl0[:], in0=tA[:], in1=tB[:],
                                               op=ALU.add))
    ch.add("vector", lambda e: e.tensor_copy(out=keptI[:], in_=kept[:]))
    ch.add("vector", lambda e: e.select(out=flat[:], mask=keptI[:], on_true=fl0[:],
                                        on_false=C["huge"][:]))
    if rt_dram is not None:
        fm_d, kp_d, pm_d = rt_dram
        identf = p.sb("rt_idf", [128, 128])
        ch.add("vector", lambda e: e.tensor_tensor(
            out=identf[:], in0=C["triu_incl"][:], in1=C["trs"][:],
            op=ALU.subtract))
        for name, sb_, d_ in (("f", flat, fm_d), ("k", kept, kp_d),
                              ("p", pmax, pm_d)):
            ch.add("tensor", lambda e, sb_=sb_: e.transpose(
                psC2[0:128, 0:128], sb_[:], identf[:]))
            ch.add("vector", lambda e: e.tensor_copy(out=tA[:],
                                                     in_=psC2[0:128, 0:128]))
            ch.dma("sync", lambda e, d_=d_: e.dma_start(
                out=bass.AP(d_, 12, [[128, 128], [1, 128]]), in_=tA[:]))
            ch.dma("sync", lambda e, d_=d_: e.dma_start(
                out=bass.AP(d_, 0, [[1, 12]]), in_=C["zeros"][0:1, 0:12]))
            ch.dma("sync", lambda e, d_=d_: e.dma_start(
                out=bass.AP(d_, 12 + 16384, [[1, 200]]), in_=C["zeros"][0:1, 0:200]))
    if slot is not None:
        slot_dram = slot
        ch.add("vector", lambda e: e.tensor_scalar(out=tA[:], in0=fl0[:],
                                                   scalar1=C["eoff"][:],
                                                   scalar2=None, op0=ALU.subtract))
        ch.add("vector", lambda e: e.tensor_scalar(out=tB[:], in0=tA[:],
                                                   scalar1=0.0, scalar2=None,
                                                   op0=ALU.is_ge))
        ch.add("vector", lambda e: e.tensor_tensor(out=tB[:], in0=tB[:], in1=kept[:],
                                                   op=ALU.mult))
        ch.add("vector", lambda e: e.select(out=tA[:], mask=tB[:], on_true=tA[:],
                                            on_false=C["huge"][:]))
        ch.add("vector", lambda e: e.tensor_copy(out=sidx[:], in_=tA[:]))
        ch.add("gpsimd", lambda e: e.iota(gid[:], [[128, 128]],
                                          channel_multiplier=1,
                                          allow_small_or_imprecise_dtypes=True))
        for t in range(128):
            ch.dma("gpsimd", lambda e, t=t: e.indirect_dma_start(
                out=slot_dram.ap(),
                out_offset=IndirectOffsetOnAxis(sidx[:, t:t + 1], 0),
                in_=gid[:, t:t + 1], in_offset=None,
                bounds_check=CAP, oob_is_err=False))


def routing_consts(p):
    C = {}
    C["triu_incl"] = p.sb("ct_tri", [128, 128])
    C["trs"] = p.sb("ct_trs", [128, 128])
    C["onescol_f"] = p.sb("ct_oc", [128, 1])
    C["onesrow_f"] = p.sb("ct_or", [1, 128])
    C["iota8"] = p.sb("ct_i8", [128, 8])
    C["eoff"] = p.sb("ct_eoff", [128, 1])
    C["huge"] = p.sb("ct_huge", [128, 128])
    C["zeros"] = p.sb("ct_z", [1, 256])
    return C


def load_routing_consts(ch, C, tri_in, trs_in, iota8_in, eoff_in):
    ch.dma("sync", lambda e: e.dma_start(out=C["triu_incl"][:], in_=tri_in))
    ch.dma("sync", lambda e: e.dma_start(out=C["trs"][:], in_=trs_in))
    ch.dma("sync", lambda e: e.dma_start(out=C["iota8"][:], in_=iota8_in))
    ch.dma("sync", lambda e: e.dma_start(out=C["eoff"][:], in_=eoff_in))
    ch.add("vector", lambda e: e.memset(C["onescol_f"][:], 1.0))
    ch.add("vector", lambda e: e.memset(C["onesrow_f"][:], 1.0))
    ch.add("vector", lambda e: e.memset(C["huge"][:], 1e9))
    ch.add("vector", lambda e: e.memset(C["zeros"][:], 0.0))



def build_d():
    p = Prog(scratch=64 * 1024)
    logits_full = p.din("logits_full", [16384, 8])
    xn2_full = p.din("xn2_full", [16384, DM])
    w1e = p.din("w1e", [DM, 4096])
    w2e = p.din("w2e", [4096, DM])
    tri_in = p.din("tri_in", [128, 128])
    trs_in = p.din("trs_in", [128, 128])
    iota8_in = p.din("iota8_in", [128, 8])
    eoff_in = p.din("eoff_in", [128, 1])
    identb = p.din("identb", [128, 128], BF16)
    ecol_in = p.din("ecol_in", [128, 1])
    eoh_in = p.din("eoh_in", [128, 8])
    pp1_in = p.din("pp1_in", [128, 2], BF16)
    t128_in = p.din("t128_in", [128, 128])
    i2d_in = p.din("i2d_in", [128, 128], BF16)

    o_eout = p.dout("o_eout", [CAP, DM])
    o_rtf = p.dout("o_rtf", [12 + 16384 + 220, 1])
    o_rtk = p.dout("o_rtk", [12 + 16384 + 220, 1])
    o_rtp = p.dout("o_rtp", [12 + 16384 + 220, 1])
    tab2_dram = p.dout("o_tab2", [16385, 1], I32).tensor
    idx2_dram = p.dout("o_idx2", [2560], I32).tensor
    hT_dram = p.dint("hT_dram", [CAP, 4096], BF16)

    identb_sb = p.sb("identb_sb", [128, 128], BF16)
    lg_all = p.sb("lg_all", [128, 1024])
    idx2_sb = p.ctx.enter_context(p.nc.sbuf_tensor("idx2_sb", [128, 20], I32))

    p.push_scope()
    blk = p.block()
    C = routing_consts(p)
    outs = {"kept": p.sb("d_kept", [128, 128]), "pmax": p.sb("d_pmax", [128, 128]),
            "flatm": p.sb("d_flat", [128, 128]),
            "rank1": p.sb("d_rank1", [128, 128])}
    ch = Chain(p)
    ch.dma("sync", lambda e: e.dma_start(out=identb_sb[:], in_=identb))
    load_routing_consts(ch, C, tri_in, trs_in, iota8_in, eoff_in)
    ch.dma("sync", lambda e: e.dma_start(
        out=lg_all[:].rearrange("p (a b) -> p a b", b=8),
        in_=bass.AP(logits_full.tensor, 0, [[8, 128], [1024, 128], [1, 8]])))
    emit_routing(p, ch, lg_all, C, outs,
                 rt_dram=(o_rtf.tensor, o_rtk.tensor, o_rtp.tensor), slot=None)

    ecol = p.sb("d_ecol", [128, 1])
    eoh = p.sb("d_eoh", [128, 8])
    pp1 = p.sb("d_pp1", [128, 2], BF16)
    t128c = p.sb("d_t128", [128, 128])
    i2d = p.sb("d_i2d", [128, 128], BF16)
    mine = p.sb("d_mine", [128, 128])
    rkb = p.sb("d_rkb", [128, 128], BF16)
    ohb = p.sb("d_ohb", [128, 128], BF16)
    tmpa = p.sb("d_tmpa", [128, 128])
    tmpb = p.sb("d_tmpb", [128, 128], I32)
    tabf = p.sb("d_tabf", [128, 128])
    tabi = p.sb("d_tabi", [128, 128], I32)
    psJ = p.ps("d_psJ", [128, 256], F32)
    ch.dma("sync", lambda e: e.dma_start(out=ecol[:], in_=ecol_in))
    ch.dma("sync", lambda e: e.dma_start(out=eoh[:], in_=eoh_in))
    ch.dma("sync", lambda e: e.dma_start(out=pp1[:], in_=pp1_in))
    ch.dma("sync", lambda e: e.dma_start(out=t128c[:], in_=t128_in))
    ch.dma("sync", lambda e: e.dma_start(out=i2d[:], in_=i2d_in))
    ch.add("vector", lambda e: e.tensor_scalar(out=mine[:], in0=outs["_idxf"][:],
                                               scalar1=ecol[:, 0:1], scalar2=None,
                                               op0=ALU.is_equal))
    ch.add("vector", lambda e: e.tensor_tensor(out=tmpa[:], in0=outs["rank1"][:],
                                               in1=mine[:], op=ALU.mult))
    ch.add("vector", lambda e: e.tensor_scalar(out=tmpa[:], in0=tmpa[:],
                                               scalar1=1.0, scalar2=None,
                                               op0=ALU.subtract))
    ch.add("vector", lambda e: e.tensor_copy(out=rkb[:], in_=tmpa[:]))
    for t in range(128):
        ch.add("vector", lambda e, t=t: e.tensor_scalar(
            out=ohb[:], in0=i2d[:], scalar1=tmpa[:, t:t + 1], scalar2=None,
            op0=ALU.is_equal))
        ch.add("tensor", lambda e, t=t: e.matmul(
            psJ[:, 2 * t:2 * t + 2], ohb[:], pp1[:], start=True, stop=True))
    v2 = lambda ap: ap.rearrange("p (a b) -> p a b", b=2)
    ch.add("vector", lambda e: e.tensor_copy(
        out=tmpa[:].rearrange("p (a b) -> p a b", b=1),
        in_=v2(psJ[:])[:, :, 0:1]))
    ch.add("vector", lambda e: e.tensor_copy(
        out=tmpb[:].rearrange("p (a b) -> p a b", b=1),
        in_=v2(psJ[:])[:, :, 1:2]))
    ch.add("vector", lambda e: e.scalar_tensor_tensor(
        out=tmpa[:], in0=tmpa[:], scalar=1.0, in1=t128c[:],
        op0=ALU.subtract, op1=ALU.add))
    ch.add("vector", lambda e: e.tensor_scalar(out=tmpb[:], in0=tmpb[:],
                                               scalar1=0.5, scalar2=None,
                                               op0=ALU.is_lt))
    ch.add("vector", lambda e: e.select(out=tabf[:], mask=tmpb[:],
                                        on_true=C["huge"][:], on_false=tmpa[:]))
    ch.add("vector", lambda e: e.tensor_copy(out=tabi[:], in_=tabf[:]))
    ch.dma("sync", lambda e: e.dma_start(
        out=bass.AP(tab2_dram, 0, [[128, 128], [1, 128]]), in_=tabi[:]))

    offcol = p.sb("d_offc", [128, 1])
    cntcol = p.sb("d_cntc", [128, 1])
    dsh = p.sb("d_dsh", [128, 1])
    t8a = p.sb("d_t8a", [128, 8])
    t8b = p.sb("d_t8b", [128, 8])
    ii = p.sb("d_ii", [128, 2560], I32)
    iof = p.sb("d_iof", [128, 2560])
    I1 = p.sb("d_I1", [128, 2560])
    u_row = p.sb("d_urow", [1, 2560])
    v_row = p.sb("d_vrow", [1, 2560])
    i2i = p.sb("d_i2i", [1, 2560], I32)
    psU = p.ps("d_psU", [128, 512], F32)
    def tree_sum8(src, dstcol):
        ch.add("vector", lambda e: e.tensor_tensor(
            out=t8a[:, 0:4], in0=src[:, 0:4], in1=src[:, 4:8], op=ALU.add))
        ch.add("vector", lambda e: e.tensor_tensor(
            out=t8a[:, 0:2], in0=t8a[:, 0:2], in1=t8a[:, 2:4], op=ALU.add))
        ch.add("vector", lambda e: e.tensor_tensor(
            out=dstcol[:], in0=t8a[:, 0:1], in1=t8a[:, 1:2], op=ALU.add))

    ch.add("vector", lambda e: e.tensor_tensor(out=t8b[:], in0=outs["_offs"][:],
                                               in1=eoh[:], op=ALU.mult))
    tree_sum8(t8b, offcol)
    ch.add("vector", lambda e: e.tensor_tensor(out=t8b[:], in0=outs["_cnt128"][:],
                                               in1=eoh[:], op=ALU.mult))
    tree_sum8(t8b, cntcol)
    ch.add("vector", lambda e: e.memset(dsh[:], 0.0))
    ch.dma("sync", lambda e: e.dma_start(out=dsh[1:128, 0:1],
                                         in_=cntcol[0:127, 0:1]))
    ch.add("gpsimd", lambda e: e.iota(ii[:], [[1, 2560]], channel_multiplier=0,
                                      allow_small_or_imprecise_dtypes=True))
    ch.add("vector", lambda e: e.tensor_copy(out=iof[:], in_=ii[:]))
    ch.add("vector", lambda e: e.tensor_scalar(out=I1[:], in0=iof[:],
                                               scalar1=offcol[:, 0:1],
                                               scalar2=None, op0=ALU.is_ge))
    for c5 in range(5):
        ch.add("tensor", lambda e, c5=c5: e.matmul(
            psU[0:1, 0:512], C["onescol_f"][:],
            I1[:, 512 * c5:512 * (c5 + 1)], start=True, stop=True))
        ch.add("vector", lambda e, c5=c5: e.tensor_copy(
            out=u_row[:, 512 * c5:512 * (c5 + 1)], in_=psU[0:1, 0:512]))
        ch.add("tensor", lambda e, c5=c5: e.matmul(
            psU[0:1, 0:512], dsh[:],
            I1[:, 512 * c5:512 * (c5 + 1)], start=True, stop=True))
        ch.add("vector", lambda e, c5=c5: e.tensor_copy(
            out=v_row[:, 512 * c5:512 * (c5 + 1)], in_=psU[0:1, 0:512]))
    ch.add("vector", lambda e: e.tensor_tensor(out=v_row[:], in0=iof[0:1, :],
                                               in1=v_row[:], op=ALU.subtract))
    ch.add("vector", lambda e: e.scalar_tensor_tensor(
        out=u_row[:], in0=v_row[:], scalar=128.0, in1=u_row[:],
        op0=ALU.mult, op1=ALU.add))
    ch.add("vector", lambda e: e.tensor_scalar(out=u_row[:], in0=u_row[:],
                                               scalar1=1.0, scalar2=None,
                                               op0=ALU.subtract))
    ch.add("vector", lambda e: e.tensor_scalar(out=u_row[:], in0=u_row[:],
                                               scalar1=16384.0, scalar2=None,
                                               op0=ALU.min))
    ch.add("vector", lambda e: e.tensor_copy(out=i2i[:], in_=u_row[:]))
    hcf = p.sb("d_hcf", [1, 1])
    hci = p.sb("d_hci", [1, 1], I32)
    ch.add("vector", lambda e: e.memset(hcf[:], 1e9))
    ch.add("vector", lambda e: e.tensor_copy(out=hci[:], in_=hcf[:]))
    ch.dma("sync", lambda e: e.dma_start(
        out=bass.AP(tab2_dram, 16384, [[1, 1]]), in_=hci[:]))
    ch.dma("sync", lambda e: e.dma_start(
        out=bass.AP(idx2_dram, 0, [[1, 2560]]), in_=i2i[:]))
    ch.dma("sync", lambda e: e.dma_start(
        out=idx2_sb[:], in_=bass.AP(idx2_dram, 0, [[20, 128], [1, 20]])))
    if DBG_D:
        o_dbgu = p.dout("o_dbgu", [1, 2560])
        o_dbgv = p.dout("o_dbgv", [1, 2560])
        o_dbgc = p.dout("o_dbgc", [128, 3])
        dbgc = p.sb("d_dbgc", [128, 3])
        ch.add("vector", lambda e: e.tensor_copy(out=dbgc[:, 0:1], in_=offcol[:]))
        ch.add("vector", lambda e: e.tensor_copy(out=dbgc[:, 1:2], in_=cntcol[:]))
        ch.add("vector", lambda e: e.tensor_copy(out=dbgc[:, 2:3], in_=dsh[:]))
        ch.dma("sync", lambda e: e.dma_start(out=o_dbgu, in_=u_row[:]))
        ch.dma("sync", lambda e: e.dma_start(out=o_dbgv, in_=v_row[:]))
        ch.dma("sync", lambda e: e.dma_start(out=o_dbgc, in_=dbgc[:]))
    ch.emit(blk)
    p.pop_scope()

    p.push_scope()
    blk = p.block()
    w1_sb = p.sb("w1_sb", [128, 8 * 4096], BF16)
    wst = [p.sb(f"d_wst1{i}", [128, 4096]) for i in range(2)]
    tids = p.sb("tids_sb", [128, 20], I32)
    tidf = p.sb("d_tidf", [128, 20])
    gx = [p.sb(f"gx{i}", [128, DM]) for i in range(2)]
    gxb = [p.sb(f"gxb{i}", [128, DM], BF16) for i in range(2)]
    gfm = [p.sb(f"gfm{i}", [128, DM], BF16) for i in range(2)]
    hT = [p.sb(f"hT_{i}", [128, 4096], BF16) for i in range(2)]
    psT = [p.ps(f"d_psT{i}", [128, 1024], BF16) for i in range(2)]
    psH = [p.ps(f"d_psH{i}", [128, 128], F32) for i in range(2)]
    ch = Chain2(p)
    for rc in range(8):
        q = rc % 2
        ch.dma("sync", lambda e, rc=rc, q=q: e.dma_start(
            out=wst[q][:], in_=w1e[128 * rc:128 * (rc + 1), :]),
            writes=(f"wst{q}",))
        ch.add("vector", lambda e, rc=rc, q=q: e.tensor_copy(
            out=w1_sb[:, 4096 * rc:4096 * (rc + 1)], in_=wst[q][:]),
            reads=(f"wst{q}",), writes=("w1",))
    ch.add("vector", lambda e: e.memset(tidf[:], 1e9), writes=("tidf",))
    ch.add("vector", lambda e: e.tensor_copy(out=tids[:], in_=tidf[:]),
           reads=("tidf",), writes=("tids",))
    for c in range(20):
        ch.dma("gpsimd", lambda e, c=c: e.indirect_dma_start(
            out=tids[:, c:c + 1], out_offset=None,
            in_=tab2_dram.ap(),
            in_offset=IndirectOffsetOnAxis(idx2_sb[:, c:c + 1], 0),
            bounds_check=16384, oob_is_err=False),
            reads=("tids",), writes=("tids",))
    for c in range(20):
        q = c % 2
        ch.add("vector", lambda e, q=q: e.memset(gx[q][:], 0.0),
               reads=(), writes=(f"gx{q}",))
        ch.dma("gpsimd", lambda e, c=c, q=q: e.indirect_dma_start(
            out=gx[q][:], out_offset=None,
            in_=xn2_full,
            in_offset=IndirectOffsetOnAxis(tids[:, c:c + 1], 0),
            bounds_check=16384, oob_is_err=False),
            reads=("tids", f"gx{q}"), writes=(f"gx{q}",))
        ch.add("vector", lambda e, q=q: e.tensor_copy(out=gxb[q][:], in_=gx[q][:]),
               reads=(f"gx{q}",), writes=(f"gxb{q}",))
        for cc in range(8):
            ch.add("tensor", lambda e, cc=cc, q=q: e.transpose(
                psT[q][:, 128 * cc:128 * (cc + 1)],
                gxb[q][:, 128 * cc:128 * (cc + 1)], identb_sb[:]),
                reads=(f"gxb{q}",), writes=(f"psT{q}",))
        ch.add("vector", lambda e, q=q: e.tensor_copy(out=gfm[q][:], in_=psT[q][:]),
               reads=(f"psT{q}",), writes=(f"gfm{q}",))
        for fo in range(32):
            fq = fo % 2
            for dc in range(8):
                ch.add("tensor", lambda e, fo=fo, dc=dc, q=q, fq=fq: e.matmul(
                    psH[fq][:, 0:128],
                    w1_sb[:, 4096 * dc + 128 * fo:4096 * dc + 128 * (fo + 1)],
                    gfm[q][:, 128 * dc:128 * (dc + 1)],
                    start=(dc == 0), stop=(dc == 7)),
                    reads=(f"gfm{q}", "w1"), writes=(f"psH{fq}",))
            ch.add("scalar", lambda e, fo=fo, q=q, fq=fq: e.activation(
                out=hT[q][:, 128 * fo:128 * (fo + 1)], in_=psH[fq][:, 0:128],
                func=AF.Relu),
                reads=(f"psH{fq}",), writes=(f"hT{q}",))
        ch.dma("sync", lambda e, c=c, q=q: e.dma_start(
            out=bass.AP(hT_dram, c * 4096, [[20 * 4096, 128], [1, 4096]]),
            in_=hT[q][:]),
            reads=(f"hT{q}",), writes=(f"hTd{c}",))
    ch.emit(blk)
    p.pop_scope()

    p.push_scope()
    blk = p.block()
    w2_sb = p.sb("w2_sb", [128, 32 * 1024], BF16)
    wst2 = [p.sb(f"d_wst2{i}", [128, 1024]) for i in range(2)]
    hT2 = [p.sb(f"hT2{i}", [128, 4096], BF16) for i in range(2)]
    eo = [p.sb(f"eo{i}", [128, DM]) for i in range(2)]
    psE = [p.ps(f"d_psE{i}", [128, 1024], F32) for i in range(2)]
    ch = Chain2(p)
    for rc in range(32):
        q = rc % 2
        ch.dma("sync", lambda e, rc=rc, q=q: e.dma_start(
            out=wst2[q][:], in_=w2e[128 * rc:128 * (rc + 1), :]),
            writes=(f"wst{q}",))
        ch.add("vector", lambda e, rc=rc, q=q: e.tensor_copy(
            out=w2_sb[:, 1024 * rc:1024 * (rc + 1)], in_=wst2[q][:]),
            reads=(f"wst{q}",), writes=("w2",))
    for c in range(20):
        q = c % 2
        ch.dma("sync", lambda e, c=c, q=q: e.dma_start(
            out=hT2[q][:],
            in_=bass.AP(hT_dram, c * 4096, [[20 * 4096, 128], [1, 4096]])),
            writes=(f"hT2{q}",))
        for half in range(2):
            for fo in range(32):
                ch.add("tensor", lambda e, half=half, fo=fo, q=q: e.matmul(
                    psE[q][:, 512 * half:512 * (half + 1)],
                    hT2[q][:, 128 * fo:128 * (fo + 1)],
                    w2_sb[:, 1024 * fo + 512 * half:1024 * fo + 512 * (half + 1)],
                    start=(fo == 0), stop=(fo == 31)),
                    reads=(f"hT2{q}", "w2"), writes=(f"psE{q}",))
        ch.add("vector", lambda e, q=q: e.tensor_copy(out=eo[q][:], in_=psE[q][:]),
               reads=(f"psE{q}",), writes=(f"eo{q}",))
        ch.dma("sync", lambda e, c=c, q=q: e.dma_start(
            out=bass.AP(o_eout.tensor, c * DM, [[20 * DM, 128], [1, DM]]),
            in_=eo[q][:]),
            reads=(f"eo{q}",), writes=(f"eod{c}",))
    ch.emit(blk)
    p.pop_scope()
    return p.done()



def build_e():
    p = Prog(scratch=40 * 1024)
    eout_full = p.din("eout_full", [8 * CAP, DM])
    x1_ext = p.din("x1_ext", [EXT, DM])
    s1_ext = p.din("s1_ext", [EXT, DM])
    xn2_ext = p.din("xn2_ext", [EXT, DM])
    rtf_in = p.din("rtf_in", [128, NT])
    rtk_in = p.din("rtk_in", [128, NT])
    rtp_in = p.din("rtp_in", [128, NT])

    o_out = p.dout("o_out", [EXT, DM])

    p.push_scope()
    blk = p.block()
    rtf = p.sb("e_rtf", [128, NT])
    rtk = p.sb("e_rtk", [128, NT])
    rtp = p.sb("e_rtp", [128, NT])
    eidx = p.sb("e_eidx", [128, NT], I32)
    eg = p.sb("e_eg", [128, DM])
    xnt = p.sb("e_xnt", [128, DM])
    x1t = p.sb("e_x1t", [128, DM])
    s1t = p.sb("e_s1t", [128, DM])
    cmb = p.sb("e_cmb", [128, DM])
    fac = p.sb("e_fac", [128, 1])
    onec = p.sb("e_onec", [128, 1])
    g1t = p.sb("e_g1t", [128, DM])
    ch = Chain(p)
    rtki = p.sb("e_rtki", [128, NT], I32)
    ch.dma("sync", lambda e: e.dma_start(out=rtf[:], in_=rtf_in))
    ch.dma("sync", lambda e: e.dma_start(out=rtk[:], in_=rtk_in))
    ch.dma("sync", lambda e: e.dma_start(out=rtp[:], in_=rtp_in))
    ch.add("vector", lambda e: e.memset(onec[:], 1.0))
    ch.add("vector", lambda e: e.tensor_copy(out=eidx[:], in_=rtf[:]))
    ch.add("vector", lambda e: e.tensor_copy(out=rtki[:], in_=rtk[:]))
    for t in range(NT):
        ch.add("vector", lambda e: e.memset(eg[:], 0.0))
        ch.dma("gpsimd", lambda e, t=t: e.indirect_dma_start(
            out=eg[:], out_offset=None, in_=eout_full,
            in_offset=IndirectOffsetOnAxis(eidx[:, t:t + 1], 0),
            bounds_check=8 * CAP, oob_is_err=False))
        ch.dma("sync", lambda e, t=t: e.dma_start(
            out=xnt[:], in_=xn2_ext[128 * t:128 * (t + 1), :]))
        ch.dma("sync", lambda e, t=t: e.dma_start(
            out=x1t[:], in_=x1_ext[128 * t:128 * (t + 1), :]))
        ch.dma("sync", lambda e, t=t: e.dma_start(
            out=s1t[:], in_=s1_ext[128 * t:128 * (t + 1), :]))
        ch.add("vector", lambda e, t=t: e.select(
            out=cmb[:], mask=rtki[:, t:t + 1].broadcast_to([128, DM]),
            on_true=eg[:], on_false=xnt[:]))
        ch.add("vector", lambda e, t=t: e.select(out=fac[:], mask=rtki[:, t:t + 1],
                                                 on_true=rtp[:, t:t + 1],
                                                 on_false=onec[:]))
        ch.add("vector", lambda e: e.tensor_scalar(out=cmb[:], in0=cmb[:],
                                                   scalar1=fac[:], scalar2=None,
                                                   op0=ALU.mult))
        ch.add("scalar", lambda e: e.activation(out=g1t[:], in_=cmb[:],
                                                func=AF.Silu))
        ch.add("vector", lambda e: e.tensor_tensor(out=g1t[:], in0=s1t[:],
                                                   in1=g1t[:], op=ALU.mult))
        ch.add("vector", lambda e: e.tensor_tensor(out=g1t[:], in0=x1t[:],
                                                   in1=g1t[:], op=ALU.add))
        ch.dma("sync", lambda e, t=t: e.dma_start(
            out=o_out[128 * t:128 * (t + 1), :], in_=g1t[:]))
    ch.emit(blk)
    p.pop_scope()
    return p.done()


def _tri_consts():
    tri = np.triu(np.ones((128, 128), np.float32))
    trs = np.triu(np.ones((128, 128), np.float32), 1)
    iota8 = np.broadcast_to(np.arange(8, np.float32) if False else
                            np.arange(8, dtype=np.float32), (128, 8))
    return tri, trs, np.ascontiguousarray(iota8)


def d_in_maps(inputs, logits_full, xn2_full):
    tri, trs, iota8 = _tri_consts()
    W1 = np.asarray(inputs["L1_ffn_W1"], np.float32)
    W2 = np.asarray(inputs["L1_ffn_W2"], np.float32)
    pp1 = np.zeros((128, 2), ml_dtypes.bfloat16)
    pp1[:, 0] = np.arange(1, 129)
    pp1[:, 1] = 1.0
    t128 = np.ascontiguousarray(np.broadcast_to(
        np.arange(128, dtype=np.float32) * 128.0, (128, 128)))
    i2d = np.ascontiguousarray(np.broadcast_to(
        np.arange(128, dtype=np.float32), (128, 128))).astype(ml_dtypes.bfloat16)
    maps = []
    for c in range(8):
        eoh = np.zeros((128, 8), np.float32)
        eoh[:, c] = 1.0
        maps.append(dict(
            logits_full=logits_full,
            xn2_full=xn2_full,
            w1e=np.ascontiguousarray(W1[c]),
            w2e=np.ascontiguousarray(W2[c]),
            tri_in=tri, trs_in=trs, iota8_in=iota8,
            eoff_in=np.full((128, 1), 2560.0 * c, np.float32),
            identb=_IDENTB,
            ecol_in=np.full((128, 1), float(c), np.float32),
            eoh_in=eoh,
            pp1_in=pp1,
            t128_in=t128,
            i2d_in=i2d,
        ))
    return maps


def e_in_maps(inputs, eout_full, rtf, rtk, rtp, b_outs, c_outs):
    maps = []
    rtf = np.asarray(rtf, np.float32).reshape(-1)
    rtk = np.asarray(rtk, np.float32).reshape(-1)
    rtp = np.asarray(rtp, np.float32).reshape(-1)

    def win(arr, st):
        return np.ascontiguousarray(arr[st:st + EXT].reshape(NT, 128).T)

    for c in range(8):
        b, j = divmod(c, 4)
        st = 8192 * b + SHIFT * j
        maps.append(dict(
            eout_full=eout_full,
            x1_ext=b_outs[c]["o_x1"],
            s1_ext=c_outs[c]["o_s1"],
            xn2_ext=b_outs[c]["o_xn2"],
            rtf_in=win(rtf, st),
            rtk_in=win(rtk, st),
            rtp_in=win(rtp, st),
        ))
    return maps


def kernel(**inputs):
    inputs = {k: np.asarray(v) for k, v in inputs.items()}
    cores = list(range(8))

    r = run_bass_kernel_spmd(build_a1(), a1_in_maps(inputs), core_ids=cores)
    a1_outs = [dict(m) for m in r.results]
    r = run_bass_kernel_spmd(build_a2(), a2_in_maps(inputs, a1_outs),
                             core_ids=cores)
    a2_outs = [dict(m) for m in r.results]
    r = run_bass_kernel_spmd(build_b(), b_in_maps(inputs, a1_outs, a2_outs),
                             core_ids=cores)
    b_outs = [dict(m) for m in r.results]
    r = run_bass_kernel_spmd(build_c(), c_in_maps(inputs, b_outs),
                             core_ids=cores)
    c_outs = [dict(m) for m in r.results]

    logits_full = np.zeros((16384, 8), np.float32)
    xn2_full = np.zeros((16384, DM), np.float32)
    for c in range(8):
        b, j = divmod(c, 4)
        Tj = SHIFT if j < 3 else S - 3 * SHIFT
        g0 = 8192 * b + SHIFT * j
        logits_full[g0:g0 + Tj] = np.asarray(
            b_outs[c]["o_logits"], np.float32)[12:12 + Tj]
        xn2_full[g0:g0 + Tj] = np.asarray(
            b_outs[c]["o_xn2"], np.float32)[12:12 + Tj]

    r = run_bass_kernel_spmd(build_d(), d_in_maps(inputs, logits_full, xn2_full),
                             core_ids=cores)
    d_outs = [dict(m) for m in r.results]
    eout_full = np.ascontiguousarray(np.concatenate(
        [np.asarray(d_outs[c]["o_eout"], np.float32) for c in range(8)], 0))
    rtf = np.asarray(d_outs[0]["o_rtf"], np.float32)
    rtk = np.asarray(d_outs[0]["o_rtk"], np.float32)
    rtp = np.asarray(d_outs[0]["o_rtp"], np.float32)

    r = run_bass_kernel_spmd(
        build_e(), e_in_maps(inputs, eout_full, rtf, rtk, rtp, b_outs, c_outs),
        core_ids=cores)
    e_outs = [dict(m) for m in r.results]

    out = np.zeros((B, S, DM), np.float32)
    for c in range(8):
        b, j = divmod(c, 4)
        Tj = SHIFT if j < 3 else S - 3 * SHIFT
        out[b, SHIFT * j:SHIFT * j + Tj] = np.asarray(
            e_outs[c]["o_out"], np.float32)[12:12 + Tj]
    return out
